# version 01
# baseline (speedup 1.0000x reference)
import contextlib
import math

import numpy as np
import ml_dtypes
import concourse.bass as bass
import concourse.mybir as mybir
from concourse.bass_utils import run_bass_kernel_spmd

F32 = mybir.dt.float32
BF16 = mybir.dt.bfloat16
AF = mybir.ActivationFunctionType
ALU = mybir.AluOpType
AX = mybir.AxisListType

S = 2048
H = 2048
NH = 16
KC = 16
NTQ = 16
EPS = 1e-5
S8 = 1.0 / math.sqrt(128.0)
INV_H = 1.0 / float(H)

GLIST = []
GBASE = {}
for _s in range(4):
    for _h in range(NH):
        GBASE[(_s, _h)] = len(GLIST)
        for _g in range(2 * (_s + 1)):
            GLIST.append((_s, _h, _g))


class _MockInstr:
    def then_inc(self, *a, **k):
        return self


class _MockEng:
    def __getattr__(self, name):
        return lambda *a, **k: _MockInstr()


class B:

    def __init__(self, sems):
        self.sems = sems
        self.mode = "plan"
        self.ticks = {}
        self.cnt = {}

    def reset(self, mode):
        self.mode = mode
        self.cnt = {k: 0 for k in self.sems}

    def inc(self, instr, sem, label, n=1):
        self.cnt[sem] += n
        if self.mode == "plan":
            self.ticks[(sem, label)] = self.cnt[sem]
        else:
            assert self.ticks.get((sem, label)) == self.cnt[sem], (sem, label)
            instr.then_inc(self.sems[sem], n)
        return self.cnt[sem]

    def w(self, eng, sem, label):
        if self.mode == "emit":
            v = self.ticks.get((sem, label))
            assert v is not None, ("missing tick", sem, label)
            eng.wait_ge(self.sems[sem], v)


def build_nc(debug=False, phases="a1,a2,a3,b,c", dump_a1=False):
    nc = bass.Bass("TRN2", target_bir_lowering=False)

    x_d = nc.dram_tensor("x", [S, H], F32, kind="ExternalInput").ap()
    wqk_d = nc.dram_tensor("wqk", [32, 128, KC, 128], F32, kind="ExternalInput").ap()
    wv_d = nc.dram_tensor("wv", [16, 128, KC, 128], F32, kind="ExternalInput").ap()
    wo_d = nc.dram_tensor("wo", [KC, 128, H], F32, kind="ExternalInput").ap()
    wffn_d = nc.dram_tensor("wffn", [KC, 128, H], F32, kind="ExternalInput").ap()
    wg_d = nc.dram_tensor("wg", [128, KC, 8], F32, kind="ExternalInput").ap()
    wa_d = nc.dram_tensor("wa", [128, KC, 128], F32, kind="ExternalInput").ap()
    wb_d = nc.dram_tensor("wb", [128, H], F32, kind="ExternalInput").ap()
    cmask_d = nc.dram_tensor("cmask", [4, 128, 512], BF16, kind="ExternalInput").ap()
    id16_d = nc.dram_tensor("id16", [128, 128], BF16, kind="ExternalInput").ap()
    id32_d = nc.dram_tensor("id32", [128, 128], F32, kind="ExternalInput").ap()
    eexp_d = nc.dram_tensor("eexp", [8, 128], F32, kind="ExternalInput").ap()
    onescol_d = nc.dram_tensor("onescol", [128, 1], BF16, kind="ExternalInput").ap()
    onesrow_d = nc.dram_tensor("onesrow", [1, 128], F32, kind="ExternalInput").ap()

    out_d = nc.dram_tensor("out", [S, H], F32, kind="ExternalOutput").ap()

    skind = "ExternalOutput" if debug else "Internal"
    if dump_a1:
        dbg1_d = nc.dram_tensor("dbg1", [128, 32768], BF16, kind="ExternalOutput").ap()
    import os as _os
    dump_b = _os.environ.get("DUMPB") == "1"
    if dump_b:
        dbgb_d = nc.dram_tensor("dbgb", [128, 32768], BF16, kind="ExternalOutput").ap()
        dbgb2_d = nc.dram_tensor("dbgb2", [128, 10240], F32, kind="ExternalOutput").ap()
        dbg2_d = nc.dram_tensor("dbg2", [128, 14336], BF16, kind="ExternalOutput").ap()
        dbg3_d = nc.dram_tensor("dbg3", [128, 32], F32, kind="ExternalOutput").ap()
    qs_d = nc.dram_tensor("qs", [NH, 128, S], BF16, kind=skind).ap()
    ks_d = nc.dram_tensor("ks", [NH, 128, S], BF16, kind=skind).ap()
    vs_d = nc.dram_tensor("vs", [NH, 16, 128, 128], BF16, kind=skind).ap()
    x1s_d = nc.dram_tensor("x1s", [NTQ, 128, H], F32, kind=skind).ap()

    with contextlib.ExitStack() as top:
        E = top.enter_context
        arena1 = E(nc.sbuf_tensor("arena1", [128, 32768], BF16))
        arena2 = E(nc.sbuf_tensor("arena2", [128, 32768], BF16))
        af32 = E(nc.sbuf_tensor("af32", [128, 10240], F32))
        a3 = E(nc.sbuf_tensor("a3", [128, 14336], BF16))
        stats = E(nc.sbuf_tensor("stats", [128, 32], F32))
        eps_sb = E(nc.sbuf_tensor("eps_sb", [128, 1], F32))
        id16 = E(nc.sbuf_tensor("id16s", [128, 128], BF16))
        id32 = E(nc.sbuf_tensor("id32s", [128, 128], F32))
        eexp = E(nc.sbuf_tensor("eexps", [8, 128], F32))
        onescol = E(nc.sbuf_tensor("onescols", [128, 1], BF16))
        onesrow = E(nc.sbuf_tensor("onesrows", [1, 128], F32))
        wa_sb = E(nc.sbuf_tensor("wa_sb", [128, 2048], BF16))
        wb_sb = E(nc.sbuf_tensor("wb_sb", [128, 2048], BF16))
        wg_sb = E(nc.sbuf_tensor("wg_sb", [128, 128], BF16))

        def xn_fm(kc):
            return arena1[:, kc * 2048:(kc + 1) * 2048]

        def attn_sl(sp, h):
            return arena1[:, sp * 8192 + h * 512: sp * 8192 + (h + 1) * 512]

        def kb(h):
            hp = h % 2
            return arena1[:, 16384 + hp * 2048: 16384 + (hp + 1) * 2048]

        def vb(h):
            hp = h % 2
            return arena1[:, 20480 + hp * 2048: 20480 + (hp + 1) * 2048]

        def qb(h):
            hp = h % 2
            return arena1[:, 24576 + hp * 512: 24576 + (hp + 1) * 512]

        def ptg(slot):
            return arena1[:, 25600 + slot * 1024: 25600 + (slot + 1) * 1024]

        def xn3(sp, kc):
            return arena1[:, sp * 8192 + kc * 512: sp * 8192 + (kc + 1) * 512]

        def w2(kc):
            return arena2[:, kc * 2048:(kc + 1) * 2048]

        def xt3(i):
            i = i % 3
            return af32[:, i * 2048:(i + 1) * 2048]

        def wo_sb(q):
            q = q % 2
            return af32[:, q * 2048:(q + 1) * 2048]

        def xtb(i):
            i = i % 2
            return af32[:, 4096 + i * 2048: 4096 + (i + 1) * 2048]

        logit_sb = af32[0:8, 8192:8704]
        rrow = af32[0:1, 8192:9216]
        lrow = af32[0:1, 9216:10240]
        cwT_sb = af32[0:8, 8704:9216]
        ge_sb = af32[:, 9216:9248]
        gt1_sb = af32[:, 9248:9280]
        gt2_sb = af32[:, 9280:9312]
        cw_sb = af32[:, 9312:9344]
        gm_sb = af32[:, 9344:9360]
        cw_er_sb = af32[:, 9728:10240]

        def slab(i):
            i = i % 2
            return a3[:, i * 2048:(i + 1) * 2048]

        def stage(i):
            i = i % 2
            return a3[:, 4096 + i * 2048: 4096 + (i + 1) * 2048]

        def xn_tm(i):
            i = i % 2
            return a3[:, 8192 + i * 2048: 8192 + (i + 1) * 2048]

        dump = a3[:, 12288:14336]

        def mask_sl(r):
            return a3[:, r * 512:(r + 1) * 512]

        def recipB(h):
            hp = h % 2
            return a3[:, 2048 + hp * 512: 2048 + (hp + 1) * 512]

        def t16_sl(sp):
            return a3[:, 4096 + sp * 512: 4096 + (sp + 1) * 512]

        def scol(base, i):
            return stats[:, base + (i % 4): base + (i % 4) + 1]

        CS, CQ, CM, CV, CR, CN, CL = 0, 4, 8, 12, 16, 20, 24

        sems = {}
        for name in ["ld", "w", "st", "pe", "act", "dve"]:
            sems[name] = E(nc.semaphore(name))
        bld = B(sems)
        blk = E(nc.Block())

        def section(eng_name, fn):
            if bld.mode == "plan":
                fn(_MockEng())
            else:
                getattr(blk, eng_name)(fn)

        def program():
            def boot_sync(e):
                for nm, dst, src in [("id16", id16, id16_d), ("id32", id32, id32_d),
                                     ("eexp", eexp, eexp_d), ("onescol", onescol, onescol_d),
                                     ("onesrow", onesrow, onesrow_d)]:
                    bld.inc(e.dma_start(out=dst[:], in_=src), "ld", ("boot", nm), 16)

            def boot_gp(e):
                bld.inc(e.dma_start(out=wa_sb[:], in_=wa_d.rearrange("p k e -> p (k e)")),
                        "w", ("boot", "wa"), 16)
                bld.inc(e.dma_start(out=wb_sb[:], in_=wb_d), "w", ("boot", "wb"), 16)
                bld.inc(e.dma_start(out=wg_sb[:], in_=wg_d.rearrange("p k e -> p (k e)")),
                        "w", ("boot", "wg"), 16)

            def boot_dve(e):
                bld.inc(e.memset(eps_sb[:], EPS), "dve", ("boot", "eps"))

            section("sync", boot_sync)
            section("gpsimd", boot_gp)
            section("vector", boot_dve)

            def fin(e):
                e.wait_ge(sems["st"], bld.cnt["st"])

            def ln_sc_sq(e, tag, i, src):
                ins = e.activation(out=dump[:], in_=src, func=AF.Square,
                                   accum_out=scol(CQ, i))
                bld.inc(ins, "act", (tag, "sq", i))

            def ln_sc_rstd(e, tag, i):
                bld.w(e, "dve", (tag, "var", i))
                e.activation(out=scol(CL, i), in_=scol(CV, i), func=AF.Ln, bias=eps_sb[:])
                ins = e.activation(out=scol(CR, i), in_=scol(CL, i), func=AF.Exp, scale=-0.5)
                bld.inc(ins, "act", (tag, "rstd", i))

            def ln_dve_sum(e, tag, i, src):
                ins = e.reduce_sum(scol(CS, i), src, axis=AX.X)
                bld.inc(ins, "dve", (tag, "sumr", i))
                bld.w(e, "dve", (tag, "sumr", i))
                ins = e.tensor_single_scalar(scol(CM, i), scol(CS, i), -INV_H, ALU.mult)
                bld.inc(ins, "dve", (tag, "m", i))

            def ln_dve_var(e, tag, i):
                bld.w(e, "act", (tag, "sq", i))
                bld.w(e, "dve", (tag, "m", i))
                ins = e.tensor_single_scalar(scol(CV, i), scol(CQ, i), INV_H, ALU.mult)
                bld.inc(ins, "dve", (tag, "v1", i))
                ins = e.tensor_mul(scol(CL, i), scol(CM, i), scol(CM, i))
                bld.inc(ins, "dve", (tag, "v2", i))
                bld.w(e, "dve", (tag, "v1", i))
                bld.w(e, "dve", (tag, "v2", i))
                ins = e.tensor_sub(scol(CV, i), scol(CV, i), scol(CL, i))
                bld.inc(ins, "dve", (tag, "var", i))

            def ln_dve_apply(e, tag, i, src, dst):
                bld.w(e, "act", (tag, "rstd", i))
                bld.w(e, "dve", (tag, "m", i))
                ins = e.tensor_scalar(out=dst, in0=src, scalar1=scol(CM, i),
                                      scalar2=scol(CR, i), op0=ALU.add, op1=ALU.mult)
                bld.inc(ins, "dve", (tag, "apply", i))

            if "a1" not in phases:
                section("sync", fin)
                return
            with contextlib.ExitStack() as ps:
                ptr = [ps.enter_context(nc.psum_tensor(f"ptr{i}_{bld.mode}", [128, 2048], BF16))
                       for i in range(2)]

                def a1_sync(e):
                    for t in range(NTQ):
                        if t >= 3:
                            bld.w(e, "dve", ("a1", "apply", t - 3))
                        ins = e.dma_start(out=xt3(t)[:, :], in_=x_d[t * 128:(t + 1) * 128, :])
                        bld.inc(ins, "ld", ("a1x", t), 16)

                def a1_sc(e):
                    bld.w(e, "dve", ("boot", "eps"))
                    for tt in range(NTQ + 1):
                        if tt <= NTQ - 1:
                            bld.w(e, "ld", ("a1x", tt))
                            ln_sc_sq(e, "a1", tt, xt3(tt)[:, :])
                        if 1 <= tt <= NTQ:
                            ln_sc_rstd(e, "a1", tt - 1)

                def a1_dve(e):
                    for tt in range(NTQ + 4):
                        if 4 <= tt:
                            t4 = tt - 4
                            bld.w(e, "pe", ("a1", "tr", t4))
                            srcp = ptr[t4 % 2][:].rearrange("p (k t) -> p k t", k=16)
                            dst = arena1[:, 0:32768].rearrange(
                                "p (k t) -> p k t", k=16)[:, :, t4 * 128:(t4 + 1) * 128]
                            ins = e.tensor_copy(out=dst, in_=srcp)
                            bld.inc(ins, "dve", ("a1", "trevac", t4))
                        if tt <= NTQ - 1:
                            bld.w(e, "ld", ("a1x", tt))
                            ln_dve_sum(e, "a1", tt, xt3(tt)[:, :])
                        if 1 <= tt <= NTQ:
                            ln_dve_var(e, "a1", tt - 1)
                        if 2 <= tt <= NTQ + 1:
                            t2 = tt - 2
                            if t2 >= 2:
                                bld.w(e, "pe", ("a1", "tr", t2 - 2))
                            ln_dve_apply(e, "a1", t2, xt3(t2)[:, :], xn_tm(t2)[:, :])

                def a1_pe(e):
                    bld.w(e, "ld", ("boot", "id16"))
                    for tt in range(2, NTQ + 2):
                        t2 = tt - 2
                        bld.w(e, "dve", ("a1", "apply", t2))
                        if t2 >= 2:
                            bld.w(e, "dve", ("a1", "trevac", t2 - 2))
                        for kc in range(KC):
                            ins = e.transpose(ptr[t2 % 2][:, kc * 128:(kc + 1) * 128],
                                              xn_tm(t2)[:, kc * 128:(kc + 1) * 128], id16[:])
                        bld.inc(ins, "pe", ("a1", "tr", t2))

                section("sync", a1_sync)
                section("scalar", a1_sc)
                section("vector", a1_dve)
                section("tensor", a1_pe)

                if dump_a1:
                    def a1_dump(e):
                        bld.w(e, "dve", ("a1", "trevac", NTQ - 1))
                        bld.inc(e.dma_start(out=dbg1_d, in_=arena1[:, :]),
                                "st", ("dbg1",), 16)
                        bld.inc(e.dma_start(out=dbg2_d, in_=a3[:, :]),
                                "st", ("dbg2",), 16)
                        bld.inc(e.dma_start(out=dbg3_d, in_=stats[:, :]),
                                "st", ("dbg3",), 16)
                    section("sync", a1_dump)

            if "a2" not in phases:
                section("sync", fin)
                return
            with contextlib.ExitStack() as ps:
                pa = [ps.enter_context(nc.psum_tensor(f"pa{i}_{bld.mode}", [128, 2048], F32))
                      for i in range(2)]

                def a2_gp(e):
                    import os
                    ser = os.environ.get("A2SER", "0") == "1"
                    for oc in range(32):
                        if ser and oc >= 1:
                            bld.w(e, "pe", ("a2", "mm", oc - 1))
                        elif oc >= 2:
                            bld.w(e, "pe", ("a2", "mm", oc - 2))
                        src = wqk_d[oc].rearrange("p k o -> p (k o)")
                        bld.inc(e.dma_start(out=slab(oc)[:, :], in_=src),
                                "w", ("a2slab", oc), 16)

                def a2_pe(e):
                    bld.w(e, "dve", ("a1", "trevac", NTQ - 1))
                    import os
                    ser = os.environ.get("A2SER", "0") == "1"
                    for oc in range(32):
                        bld.w(e, "w", ("a2slab", oc))
                        if ser and oc >= 1:
                            bld.w(e, "act", ("a2", "evac", oc - 1))
                        elif oc >= 2:
                            bld.w(e, "act", ("a2", "evac", oc - 2))
                        for st4 in range(4):
                            for kc in range(KC):
                                ins = e.matmul(pa[oc % 2][:, st4 * 512:(st4 + 1) * 512],
                                               slab(oc)[:, kc * 128:(kc + 1) * 128],
                                               xn_fm(kc)[:, st4 * 512:(st4 + 1) * 512],
                                               start=(kc == 0), stop=(kc == KC - 1))
                        bld.inc(ins, "pe", ("a2", "mm", oc))

                def a2_sc(e):
                    for oc in range(32):
                        bld.w(e, "pe", ("a2", "mm", oc))
                        if oc >= 2:
                            bld.w(e, "st", ("a2st", oc - 2))
                        ins = e.activation(out=stage(oc)[:, :], in_=pa[oc % 2][:],
                                           func=AF.Copy)
                        bld.inc(ins, "act", ("a2", "evac", oc))
                        bld.w(e, "act", ("a2", "evac", oc))
                        dst = qs_d[oc] if oc < 16 else ks_d[oc - 16]
                        bld.inc(e.dma_start(out=dst, in_=stage(oc)[:, :]),
                                "st", ("a2st", oc), 16)

                section("gpsimd", a2_gp)
                section("tensor", a2_pe)
                section("scalar", a2_sc)

            if "a3" not in phases:
                section("sync", fin)
                return
            with contextlib.ExitStack() as ps:
                pa3 = [ps.enter_context(nc.psum_tensor(f"pa3{i}_{bld.mode}", [128, 1024], F32))
                       for i in range(2)]
                vtr = [ps.enter_context(nc.psum_tensor(f"vtr{i}_{bld.mode}", [128, 2048], BF16))
                       for i in range(2)]

                def a3_gp(e):
                    for dc in range(16):
                        if dc < 2:
                            bld.w(e, "pe", ("a2", "mm", 30 + dc))
                        else:
                            bld.w(e, "pe", ("a3", "mm", dc - 2, 1))
                        src = wv_d[dc].rearrange("p k o -> p (k o)")
                        bld.inc(e.dma_start(out=slab(dc)[:, :], in_=src),
                                "w", ("a3slab", dc), 16)
                    for kc in range(KC):
                        bld.inc(e.dma_start(out=w2(kc)[:, :], in_=wo_d[kc]),
                                "w", ("wo", kc), 16)

                def a3_pe(e):
                    bld.w(e, "act", ("a2", "evac", 31))
                    for dcx in range(17):
                        if dcx <= 15:
                            dc = dcx
                            bld.w(e, "w", ("a3slab", dc))
                            for sub in range(2):
                                gsub = dc * 2 + sub
                                gg = gsub % 2
                                if gsub >= 2:
                                    bld.w(e, "act", ("a3", "evac", (gsub - 2) // 2,
                                                     (gsub - 2) % 2))
                                for kc in range(KC):
                                    for st in range(2):
                                        ins = e.matmul(
                                            pa3[gg][:, st * 512:(st + 1) * 512],
                                            slab(dc)[:, kc * 128:(kc + 1) * 128],
                                            xn_fm(kc)[:, (sub * 2 + st) * 512:(sub * 2 + st + 1) * 512],
                                            start=(kc == 0), stop=(kc == KC - 1))
                                bld.inc(ins, "pe", ("a3", "mm", dc, sub))
                        if dcx >= 1:
                            dp = dcx - 1
                            bld.w(e, "act", ("a3", "evac", dp, 1))
                            if dp >= 2:
                                bld.w(e, "act", ("a3", "vtrevac", dp - 2))
                            for tkc in range(16):
                                ins = e.transpose(vtr[dp % 2][:, tkc * 128:(tkc + 1) * 128],
                                                  stage(dp)[:, tkc * 128:(tkc + 1) * 128],
                                                  id16[:])
                            bld.inc(ins, "pe", ("a3", "tr", dp))

                def a3_sc(e):
                    for dcx in range(17):
                        if dcx <= 15:
                            dc = dcx
                            for sub in range(2):
                                bld.w(e, "pe", ("a3", "mm", dc, sub))
                                ins = e.activation(
                                    out=stage(dc)[:, sub * 1024:(sub + 1) * 1024],
                                    in_=pa3[(dc * 2 + sub) % 2][:], func=AF.Copy)
                                bld.inc(ins, "act", ("a3", "evac", dc, sub))
                        if dcx >= 1:
                            dp = dcx - 1
                            bld.w(e, "pe", ("a3", "tr", dp))
                            if dp >= 2:
                                bld.w(e, "st", ("a3vst", dp - 2, 3))
                            ins = e.activation(out=xn_tm(dp)[:, :], in_=vtr[dp % 2][:],
                                               func=AF.Copy)
                            bld.inc(ins, "act", ("a3", "vtrevac", dp))
                            bld.w(e, "act", ("a3", "vtrevac", dp))
                            for g in range(4):
                                dst = vs_d[dp, 4 * g:4 * g + 4].rearrange("c p d -> p c d")
                                srcv = xn_tm(dp)[:, g * 512:(g + 1) * 512].rearrange(
                                    "p (c d) -> p c d", c=4)
                                ins = e.dma_start(out=dst, in_=srcv)
                                bld.inc(ins, "st", ("a3vst", dp, g), 16)

                section("gpsimd", a3_gp)
                section("tensor", a3_pe)
                section("scalar", a3_sc)

            import os
            BL = int(os.environ.get("BL", "6"))
            AV_OFF = (BL == 31)
            EVAC_OFF = (BL in (31, 32))
            if BL in (31, 32):
                BL = 3
            BMAXS = int(os.environ.get("BMAXS", "4"))
            BMAXH = int(os.environ.get("BMAXH", str(NH)))
            if "b" not in phases:
                section("sync", fin)
                return
            with contextlib.ExitStack() as ps:
                pscW = ps.enter_context(nc.psum_tensor("pscW_" + bld.mode, [128, 2048], F32))
                pav = [ps.enter_context(nc.psum_tensor(f"pav{i}_{bld.mode}", [128, 512], F32))
                       for i in range(2)]
                psl = ps.enter_context(nc.psum_tensor("psl_" + bld.mode, [1, 512], F32))
                psb = ps.enter_context(nc.psum_tensor("psb_" + bld.mode, [128, 512], F32))

                psc_last = [None, None]

                def b_sync(e, s):
                    if s >= BMAXS:
                        return
                    nblk = 4 * (s + 1)
                    if s == 0:
                        bld.w(e, "pe", ("a3", "tr", 15))
                        ins = e.dma_start(out=a3[:, 0:2048].rearrange(
                                              "p (r j) -> p r j", r=4),
                                          in_=cmask_d.rearrange("r p j -> p r j"))
                        bld.inc(ins, "ld", ("masks",), 16)
                    for h in range(min(NH, BMAXH)):
                        gh = s * NH + h
                        if gh >= 2:
                            ps_, hh = divmod(gh - 2, NH)
                            ng = 2 * (ps_ + 1)
                            if BL >= 3:
                                bld.w(e, "pe", ("b", "lav", ps_, hh, ng - 1))
                            else:
                                bld.w(e, "pe", ("b", "sc", ps_, hh, ng - 1))
                        else:
                            bld.w(e, "pe", ("a3", "mm", 15, 1))
                        ins = e.dma_start(out=kb(h)[:, 0:nblk * 128],
                                          in_=ks_d[h][:, 0:nblk * 128])
                        bld.inc(ins, "ld", ("kb", s, h), 16)
                        ins = e.dma_start(out=qb(h)[:, :],
                                          in_=qs_d[h][:, s * 512:(s + 1) * 512])
                        bld.inc(ins, "ld", ("qb", s, h), 16)
                        if BL >= 3:
                            for c in range(nblk):
                                ins = e.dma_start(out=vb(h)[:, c * 128:(c + 1) * 128],
                                                  in_=vs_d[h, c])
                                bld.inc(ins, "ld", ("vb", s, h, c), 16)
                            bld.ticks[("ld", ("vb", s, h))] = bld.cnt["ld"]
                    if BL < 6:
                        return
                    for q in range(4):
                        gq = s * 4 + q
                        if gq >= 2:
                            bld.w(e, "dve", ("b2", "x1add", (gq - 2) // 4, (gq - 2) % 4))
                            bld.w(e, "st", ("b2", "x1st", (gq - 2) // 4, (gq - 2) % 4))
                        else:
                            bld.w(e, "dve", ("a1", "apply", NTQ - 1))
                        row = s * 512 + q * 128
                        ins = e.dma_start(out=xtb(gq)[:, :], in_=x_d[row:row + 128, :])
                        bld.inc(ins, "ld", ("xb2", s, q), 16)

                def b_pe(e, s):
                    if s >= BMAXS:
                        return
                    nblk = 4 * (s + 1)
                    NG = nblk // 2
                    if s == 0:
                        bld.w(e, "act", ("a3", "evac", 15, 1))
                        bld.w(e, "act", ("a3", "vtrevac", 15))
                    for h in range(min(NH, BMAXH)):
                        gh = s * NH + h
                        base = GBASE[(s, h)]
                        bld.w(e, "ld", ("kb", s, h))
                        bld.w(e, "ld", ("qb", s, h))
                        do_av = (BL >= 3) and not AV_OFF
                        for gx in range(NG + 2 if do_av else NG):
                            if gx < NG:
                                g = gx
                                gg = g % 2
                                if psc_last[gg] is not None:
                                    bld.w(e, "act", psc_last[gg])
                                for c2 in range(2):
                                    c = 2 * g + c2
                                    ins = e.matmul(
                                        pscW[:, gg * 1024 + c2 * 512: gg * 1024 + (c2 + 1) * 512],
                                        kb(h)[:, c * 128:(c + 1) * 128],
                                        qb(h)[:, :], start=True, stop=True)
                                bld.inc(ins, "pe", ("b", "sc", s, h, g))
                                psc_last[gg] = ("b", "exp", s, h, g)
                            if do_av and gx >= 2:
                                g = gx - 2
                                if g == 0:
                                    bld.w(e, "ld", ("vb", s, h))
                                    if gh >= 1 and BL >= 4:
                                        ph = gh - 1
                                        bld.w(e, "act", ("b", "lev", ph // NH, ph % NH))
                                    if gh >= 2:
                                        ph = gh - 2
                                        bld.w(e, "act", ("b", "attnev", ph // NH, ph % NH))
                                diag = (g >= NG - 2)
                                if diag and BL >= 2:
                                    bld.w(e, "dve", ("b", "mask", s, h, g))
                                else:
                                    bld.w(e, "act", ("b", "exp", s, h, g))
                                slot = (base + g) % 4
                                for c2 in range(2):
                                    c = 2 * g + c2
                                    if BL >= 4:
                                        e.matmul(psl[:, :], onescol[:],
                                                 ptg(slot)[:, c2 * 512:(c2 + 1) * 512],
                                                 start=(c == 0), stop=(c == nblk - 1))
                                    ins = e.matmul(pav[h % 2][:, :],
                                                   vb(h)[:, c * 128:(c + 1) * 128],
                                                   ptg(slot)[:, c2 * 512:(c2 + 1) * 512],
                                                   start=(c == 0), stop=(c == nblk - 1))
                                bld.inc(ins, "pe", ("b", "lav", s, h, g))
                        if not do_av:
                            if AV_OFF and bld.mode == "plan":
                                for g in range(NG):
                                    bld.ticks[("pe", ("b", "lav", s, h, g))] = \
                                        bld.ticks[("pe", ("b", "sc", s, h, g))]
                            continue
                        if BL < 4:
                            continue
                        bld.w(e, "dve", ("b", "recip", s, h))
                        if gh >= 1:
                            ph = gh - 1
                            bld.w(e, "act", ("b", "rbevac", ph // NH, ph % NH))
                        ins = e.matmul(psb[:, :], onesrow[:],
                                       rrow[:, (h % 2) * 512:((h % 2) + 1) * 512],
                                       start=True, stop=True)
                        bld.inc(ins, "pe", ("b", "bc", s, h))
                    if BL < 5:
                        return
                    bld.w(e, "dve", ("b", "amul", s, NH - 1))
                    if s == 0:
                        bld.w(e, "w", ("wo", KC - 1))
                    for q in range(4):
                        for gg in range(2):
                            if psc_last[gg] is not None:
                                bld.w(e, "act", psc_last[gg])
                        for kc in range(KC):
                            for os_ in range(4):
                                ins = e.matmul(
                                    pscW[:, os_ * 512:(os_ + 1) * 512],
                                    attn_sl(s % 2, kc)[:, q * 128:(q + 1) * 128],
                                    w2(kc)[:, os_ * 512:(os_ + 1) * 512],
                                    start=(kc == 0), stop=(kc == KC - 1))
                        bld.inc(ins, "pe", ("b", "womm", s, q))
                        if BL >= 6:
                            psc_last[0] = ("b2", "evac", s, q)
                            psc_last[1] = ("b2", "evac", s, q)

                def b_sc(e, s):
                    if s >= BMAXS:
                        return
                    nblk = 4 * (s + 1)
                    NG = nblk // 2
                    for h in range(min(NH, BMAXH)):
                        gh = s * NH + h
                        base = GBASE[(s, h)]
                        if h == 0 and s >= 2 and BL >= 5:
                            bld.w(e, "pe", ("b", "womm", s - 2, 3))
                        for g in range(NG):
                            bld.w(e, "pe", ("b", "sc", s, h, g))
                            gi = base + g
                            if gi >= 4 and BL >= 3:
                                bld.w(e, "pe", ("b", "lav") + GLIST[gi - 4])
                            slot = gi % 4
                            gg = g % 2
                            ins = e.activation(out=ptg(slot)[:, :],
                                               in_=pscW[:, gg * 1024:(gg + 1) * 1024],
                                               func=AF.Exp, scale=S8)
                            bld.inc(ins, "act", ("b", "exp", s, h, g))
                        if BL < 3 or EVAC_OFF:
                            if bld.mode == "plan" and BL >= 3:
                                bld.ticks[("act", ("b", "attnev", s, h))] = \
                                    bld.ticks[("act", ("b", "exp", s, h, NG - 1))]
                            continue
                        bld.w(e, "pe", ("b", "lav", s, h, NG - 1))
                        if BL >= 4:
                            ins = e.activation(out=lrow[:, (h % 2) * 512:((h % 2) + 1) * 512],
                                               in_=psl[:], func=AF.Copy)
                            bld.inc(ins, "act", ("b", "lev", s, h))
                        ins = e.activation(out=attn_sl(s % 2, h)[:, :], in_=pav[h % 2][:],
                                           func=AF.Copy)
                        bld.inc(ins, "act", ("b", "attnev", s, h))
                        if BL < 4:
                            continue
                        bld.w(e, "pe", ("b", "bc", s, h))
                        ins = e.activation(out=recipB(h)[:, :], in_=psb[:], func=AF.Copy)
                        bld.inc(ins, "act", ("b", "rbevac", s, h))
                    if BL < 6:
                        return
                    for qq in range(5):
                        if qq <= 3:
                            q = qq
                            bld.w(e, "pe", ("b", "womm", s, q))
                            gq = s * 4 + q
                            if gq >= 2:
                                bld.w(e, "dve", ("b2", "x1add", (gq - 2) // 4, (gq - 2) % 4))
                            ins = e.activation(out=wo_sb(q)[:, :], in_=pscW[:],
                                               func=AF.Copy)
                            bld.inc(ins, "act", ("b2", "evac", s, q))
                            ln_sc_sq(e, ("b2", s), q, wo_sb(q)[:, :])
                        if qq >= 1:
                            q = qq - 1
                            ln_sc_rstd(e, ("b2", s), q)
                            bld.w(e, "dve", ("b2", "x1add", s, q))
                            gq = s * 4 + q
                            ins = e.dma_start(out=x1s_d[gq], in_=xtb(gq)[:, :])
                            bld.inc(ins, "st", ("b2", "x1st", s, q), 16)

                def b_dve(e, s):
                    if s >= BMAXS:
                        return
                    nblk = 4 * (s + 1)
                    NG = nblk // 2
                    if BL < 2:
                        return
                    if s == 0:
                        bld.w(e, "ld", ("masks",))
                    for h in range(min(NH, BMAXH)):
                        gh = s * NH + h
                        base = GBASE[(s, h)]
                        for g in range(NG - 2, NG):
                            bld.w(e, "act", ("b", "exp", s, h, g))
                            slot = (base + g) % 4
                            ins = None
                            for c2 in range(2):
                                c = 2 * g + c2
                                r = c - 4 * s
                                if 0 <= r <= 3:
                                    pt_t = ptg(slot)[:, c2 * 512:(c2 + 1) * 512]
                                    ins = e.tensor_mul(pt_t, pt_t, mask_sl(r)[:, :])
                            bld.inc(ins, "dve", ("b", "mask", s, h, g))
                        if BL < 4:
                            continue
                        bld.w(e, "act", ("b", "lev", s, h))
                        if gh >= 2:
                            ph = gh - 2
                            bld.w(e, "pe", ("b", "bc", ph // NH, ph % NH))
                        ins = e.reciprocal(rrow[:, (h % 2) * 512:((h % 2) + 1) * 512],
                                           lrow[:, (h % 2) * 512:((h % 2) + 1) * 512])
                        bld.inc(ins, "dve", ("b", "recip", s, h))
                        bld.w(e, "act", ("b", "attnev", s, h))
                        bld.w(e, "act", ("b", "rbevac", s, h))
                        at = attn_sl(s % 2, h)[:, :]
                        ins = e.tensor_mul(at, at, recipB(h)[:, :])
                        bld.inc(ins, "dve", ("b", "amul", s, h))
                    if BL < 6:
                        return
                    for qq in range(5):
                        if qq <= 3:
                            q = qq
                            bld.w(e, "act", ("b2", "evac", s, q))
                            ln_dve_sum(e, ("b2", s), q, wo_sb(q)[:, :])
                            ln_dve_var(e, ("b2", s), q)
                        if qq >= 1:
                            q = qq - 1
                            ln_dve_apply(e, ("b2", s), q, wo_sb(q)[:, :], wo_sb(q)[:, :])
                            bld.w(e, "ld", ("xb2", s, q))
                            gq = s * 4 + q
                            ins = e.tensor_add(xtb(gq)[:, :], xtb(gq)[:, :], wo_sb(q)[:, :])
                            bld.inc(ins, "dve", ("b2", "x1add", s, q))

                for s in range(4):
                    section("sync", lambda e, s=s: b_sync(e, s))
                    section("tensor", lambda e, s=s: b_pe(e, s))
                    section("scalar", lambda e, s=s: b_sc(e, s))
                    section("vector", lambda e, s=s: b_dve(e, s))

                if dump_b:
                    def b_dump(e):
                        bld.w(e, "pe", ("b", "womm", 3, 3))
                        bld.w(e, "dve", ("b", "amul", 3, NH - 1))
                        bld.inc(e.dma_start(out=dbgb_d, in_=arena1[:, :]),
                                "st", ("dbgb",), 16)
                        bld.inc(e.dma_start(out=dbgb2_d, in_=af32[:, :]),
                                "st", ("dbgb2",), 16)
                    section("sync", b_dump)

                def b_gp(e):
                    if BL >= 5:
                        bld.w(e, "pe", ("b", "womm", 3, 3))
                    else:
                        ls = BMAXS - 1
                        bld.w(e, "pe", ("b", "sc", ls, min(NH, BMAXH) - 1, 2 * (ls + 1) - 1))
                    for kc in range(KC):
                        bld.inc(e.dma_start(out=w2(kc)[:, :], in_=wffn_d[kc]),
                                "w", ("wffn", kc), 16)

                section("gpsimd", b_gp)

            CMAXS = int(os.environ.get("CMAXS", "4"))
            if "c" not in phases:
                section("sync", fin)
                return
            with contextlib.ExitStack() as ps:
                pW = ps.enter_context(nc.psum_tensor("pW_" + bld.mode, [128, 2048], F32))
                psA = ps.enter_context(nc.psum_tensor("psA_" + bld.mode, [128, 512], F32))
                pmisc = ps.enter_context(nc.psum_tensor("pmisc_" + bld.mode, [128, 512], F32))
                trC = ps.enter_context(nc.psum_tensor("trC_" + bld.mode, [128, 2048], BF16))

                def c_sync(e, s4):
                    if s4 >= CMAXS:
                        return
                    for q in range(4):
                        gq = s4 * 4 + q
                        if q >= 2:
                            bld.w(e, "dve", (("c3", s4), "apply", q - 2))
                        elif s4 >= 1:
                            bld.w(e, "dve", ("c4", "outadd", s4 - 1, q + 2))
                            bld.w(e, "st", ("out", s4 - 1, q + 2))
                        else:
                            bld.w(e, "st", ("b2", "x1st", 3, 3))
                        ins = e.dma_start(out=xtb(gq)[:, :], in_=x1s_d[gq])
                        bld.inc(ins, "ld", ("x1a", s4, q), 16)
                    for q in range(4):
                        gq = s4 * 4 + q
                        bld.w(e, "dve", (("c3", s4), "apply", 3))
                        if q >= 2:
                            bld.w(e, "dve", ("c4", "outadd", s4, q - 2))
                            bld.w(e, "st", ("out", s4, q - 2))
                        ins = e.dma_start(out=xtb(gq)[:, :], in_=x1s_d[gq])
                        bld.inc(ins, "ld", ("x1b", s4, q), 16)

                def c_pe(e, s4):
                    if s4 >= CMAXS:
                        return
                    sp = s4 % 2
                    if s4 == 0:
                        bld.w(e, "w", ("wffn", KC - 1))
                        bld.w(e, "w", ("boot", "wa"))
                        bld.w(e, "w", ("boot", "wb"))
                        bld.w(e, "w", ("boot", "wg"))
                        bld.w(e, "ld", ("boot", "id32"))
                        bld.w(e, "ld", ("boot", "eexp"))
                        bld.w(e, "act", ("b2", "evac", 3, 3))
                        bld.w(e, "act", ("b", "attnev", 3, 15))
                        bld.w(e, "act", ("b", "rbevac", 3, 15))
                        bld.w(e, "dve", ("b", "recip", 3, 15))
                    for q in range(4):
                        bld.w(e, "dve", (("c3", s4), "apply", q))
                        gq = s4 * 4 + q
                        if gq >= 1:
                            pq = gq - 1
                            bld.w(e, "dve", ("c3", "trevac", pq // 4, pq % 4))
                        for kc in range(KC):
                            ins = e.transpose(trC[:, kc * 128:(kc + 1) * 128],
                                              xn_tm(q)[:, kc * 128:(kc + 1) * 128],
                                              id16[:])
                        bld.inc(ins, "pe", ("c3", "tr", s4, q))
                    bld.w(e, "dve", ("c3", "trevac", s4, 3))
                    if s4 >= 1:
                        bld.w(e, "dve", ("c", "t16", s4 - 1))
                        bld.w(e, "act", ("c", "cwTevac", s4 - 1))
                    for kc in range(KC):
                        e.matmul(psA[:, :], wa_sb[:, kc * 128:(kc + 1) * 128],
                                 xn3(sp, kc)[:, :],
                                 start=(kc == 0), stop=(kc == KC - 1))
                        ins = e.matmul(pmisc[0:8, :], wg_sb[:, kc * 8:(kc + 1) * 8],
                                       xn3(sp, kc)[:, :],
                                       start=(kc == 0), stop=(kc == KC - 1))
                    bld.inc(ins, "pe", ("c", "ag", s4))
                    bld.w(e, "act", ("c", "logevac", s4))
                    if s4 >= 1:
                        bld.w(e, "act", ("c", "geexp", s4 - 1))
                    for q in range(4):
                        ins = e.transpose(pmisc[:, q * 8:(q + 1) * 8],
                                          logit_sb[:, q * 128:(q + 1) * 128],
                                          id32[0:8, 0:8])
                    bld.inc(ins, "pe", ("c", "tr1", s4))
                    bld.w(e, "dve", ("c", "gat", s4))
                    for q in range(4):
                        ins = e.transpose(pmisc[0:8, q * 128:(q + 1) * 128],
                                          cw_sb[:, q * 8:(q + 1) * 8], id32[:])
                    bld.inc(ins, "pe", ("c", "tr2", s4))
                    bld.w(e, "act", ("c", "cwTevac", s4))
                    if s4 >= 1:
                        bld.w(e, "act", ("c4", "evac", s4 - 1, 3))
                    for q in range(4):
                        ins = e.matmul(pW[:, q * 128:(q + 1) * 128], eexp[:],
                                       cwT_sb[:, q * 128:(q + 1) * 128],
                                       start=True, stop=True)
                    bld.inc(ins, "pe", ("c", "exp", s4))
                    for q in range(4):
                        if q == 0:
                            bld.w(e, "act", ("c", "cwerevac", s4))
                            bld.w(e, "dve", ("c", "t16", s4))
                        else:
                            bld.w(e, "act", ("c4", "evac", s4, q - 1))
                        for kc in range(KC):
                            for os_ in range(4):
                                e.matmul(pW[:, os_ * 512:(os_ + 1) * 512],
                                         xn3(sp, kc)[:, q * 128:(q + 1) * 128],
                                         w2(kc)[:, os_ * 512:(os_ + 1) * 512],
                                         start=(kc == 0), stop=False)
                        for os_ in range(4):
                            ins = e.matmul(pW[:, os_ * 512:(os_ + 1) * 512],
                                           t16_sl(sp)[:, q * 128:(q + 1) * 128],
                                           wb_sb[:, os_ * 512:(os_ + 1) * 512],
                                           start=False, stop=True)
                        bld.inc(ins, "pe", ("c4", "mm", s4, q))

                def c_sc(e, s4):
                    if s4 >= CMAXS:
                        return
                    for qq in range(5):
                        if qq <= 3:
                            q = qq
                            bld.w(e, "ld", ("x1a", s4, q))
                            if s4 == 0 and q == 0:
                                bld.w(e, "st", ("a3vst", 15, 3))
                            ln_sc_sq(e, ("c3", s4), q, xtb(s4 * 4 + q)[:, :])
                        if 1 <= qq <= 4:
                            ln_sc_rstd(e, ("c3", s4), qq - 1)
                    bld.w(e, "pe", ("c", "ag", s4))
                    ins = e.activation(out=logit_sb[:, :], in_=pmisc[0:8, :], func=AF.Copy)
                    bld.inc(ins, "act", ("c", "logevac", s4))
                    bld.w(e, "pe", ("c", "tr1", s4))
                    ins = e.activation(out=ge_sb[:, :], in_=pmisc[:, 0:32], func=AF.Exp)
                    bld.inc(ins, "act", ("c", "geexp", s4))
                    bld.w(e, "pe", ("c", "tr2", s4))
                    ins = e.activation(out=cwT_sb[:, :], in_=pmisc[0:8, :], func=AF.Copy)
                    bld.inc(ins, "act", ("c", "cwTevac", s4))
                    bld.w(e, "pe", ("c", "exp", s4))
                    ins = e.activation(out=cw_er_sb[:, :], in_=pW[:, 0:512], func=AF.Copy)
                    bld.inc(ins, "act", ("c", "cwerevac", s4))
                    for qq in range(5):
                        if qq <= 3:
                            q = qq
                            bld.w(e, "pe", ("c4", "mm", s4, q))
                            ins = e.activation(out=wo_sb(q)[:, :], in_=pW[:], func=AF.Copy)
                            bld.inc(ins, "act", ("c4", "evac", s4, q))
                            ln_sc_sq(e, ("c4", s4), q, wo_sb(q)[:, :])
                        if qq >= 1:
                            q = qq - 1
                            ln_sc_rstd(e, ("c4", s4), q)
                            bld.w(e, "dve", ("c4", "outadd", s4, q))
                            gq = s4 * 4 + q
                            row = gq * 128
                            ins = e.dma_start(out=out_d[row:row + 128, :],
                                              in_=xtb(gq)[:, :])
                            bld.inc(ins, "st", ("out", s4, q), 16)

                def c_dve(e, s4):
                    if s4 >= CMAXS:
                        return
                    sp = s4 % 2
                    if s4 == 0:
                        bld.w(e, "pe", ("b", "womm", 3, 3))
                    for qq in range(6):
                        if 2 <= qq <= 5:
                            q = qq - 2
                            gq = s4 * 4 + q
                            if gq >= 2:
                                pq = gq - 2
                                bld.w(e, "pe", ("c3", "tr", pq // 4, pq % 4))
                            ln_dve_apply(e, ("c3", s4), q, xtb(s4 * 4 + q)[:, :],
                                         xn_tm(q)[:, :])
                        if qq <= 3:
                            q = qq
                            bld.w(e, "ld", ("x1a", s4, q))
                            ln_dve_sum(e, ("c3", s4), q, xtb(s4 * 4 + q)[:, :])
                        if 1 <= qq <= 4:
                            ln_dve_var(e, ("c3", s4), qq - 1)
                        if 2 <= qq:
                            q = qq - 2
                            bld.w(e, "pe", ("c3", "tr", s4, q))
                            if s4 >= 2 and q == 0:
                                bld.w(e, "pe", ("c4", "mm", s4 - 2, 3))
                            srcp = trC[:].rearrange("p (k t) -> p k t", k=16)
                            dst = arena1[:, sp * 8192:(sp + 1) * 8192].rearrange(
                                "p (k t) -> p k t", k=16)[:, :, q * 128:(q + 1) * 128]
                            ins = e.tensor_copy(out=dst, in_=srcp)
                            bld.inc(ins, "dve", ("c3", "trevac", s4, q))
                    bld.w(e, "act", ("c", "geexp", s4))
                    for q in range(4):
                        ge = ge_sb[:, q * 8:(q + 1) * 8]
                        gm1 = gm_sb[:, q:q + 1]
                        gm2 = gm_sb[:, 4 + q:5 + q]
                        gden = gm_sb[:, 8 + q:9 + q]
                        t1 = gt1_sb[:, q * 8:(q + 1) * 8]
                        t2 = gt2_sb[:, q * 8:(q + 1) * 8]
                        cw = cw_sb[:, q * 8:(q + 1) * 8]
                        def gstep(ins_, lbl):
                            bld.inc(ins_, "dve", ("c", lbl, s4, q))
                            bld.w(e, "dve", ("c", lbl, s4, q))
                        gstep(e.reduce_max(gm1, ge, axis=AX.X), "g1")
                        gstep(e.tensor_scalar(out=t1, in0=ge, scalar1=gm1, scalar2=None,
                                              op0=ALU.is_equal), "g1b")
                        gstep(e.tensor_single_scalar(t1, t1, 1e30, ALU.mult), "g1c")
                        gstep(e.tensor_sub(t1, ge, t1), "g1d")
                        gstep(e.reduce_max(gm2, t1, axis=AX.X), "g2")
                        gstep(e.tensor_scalar(out=t2, in0=ge, scalar1=gm2, scalar2=None,
                                              op0=ALU.is_ge), "g2b")
                        gstep(e.tensor_mul(cw, ge, t2), "g2c")
                        gstep(e.reduce_sum(gden, cw, axis=AX.X), "g3")
                        gstep(e.reciprocal(gden, gden), "g4")
                        ins = e.tensor_scalar(out=cw, in0=cw, scalar1=gden, scalar2=None,
                                              op0=ALU.mult)
                    bld.inc(ins, "dve", ("c", "gat", s4))
                    bld.w(e, "act", ("c", "cwerevac", s4))
                    ins = e.tensor_tensor(out=t16_sl(sp)[:, :], in0=psA[:, :],
                                          in1=cw_er_sb[:, :], op=ALU.mult)
                    bld.inc(ins, "dve", ("c", "t16", s4))
                    for qq in range(5):
                        if qq <= 3:
                            q = qq
                            bld.w(e, "act", ("c4", "evac", s4, q))
                            ln_dve_sum(e, ("c4", s4), q, wo_sb(q)[:, :])
                            ln_dve_var(e, ("c4", s4), q)
                        if qq >= 1:
                            q = qq - 1
                            ln_dve_apply(e, ("c4", s4), q, wo_sb(q)[:, :], wo_sb(q)[:, :])
                            bld.w(e, "ld", ("x1b", s4, q))
                            gq = s4 * 4 + q
                            ins = e.tensor_add(xtb(gq)[:, :], xtb(gq)[:, :],
                                               wo_sb(q)[:, :])
                            bld.inc(ins, "dve", ("c4", "outadd", s4, q))

                for s4 in range(4):
                    section("sync", lambda e, s4=s4: c_sync(e, s4))
                    section("tensor", lambda e, s4=s4: c_pe(e, s4))
                    section("scalar", lambda e, s4=s4: c_sc(e, s4))
                    section("vector", lambda e, s4=s4: c_dve(e, s4))

            section("sync", fin)

        bld.reset("plan")
        program()
        bld.reset("emit")
        program()

    return nc


def _prep(inputs):
    wqkv = np.ascontiguousarray(inputs["L0_attn_w_qkv"], dtype=np.float32)
    wo = np.ascontiguousarray(inputs["L0_attn_w_o"], dtype=np.float32)
    wffn = np.ascontiguousarray(inputs["L0_ffn_weight"], dtype=np.float32)
    wg = np.ascontiguousarray(inputs["L0_ffn_gate_w"], dtype=np.float32)
    wa = np.ascontiguousarray(inputs["L0_ffn_A_w"], dtype=np.float32)
    wb = np.ascontiguousarray(inputs["L0_ffn_B_w"], dtype=np.float32)

    d = {}
    d["wqk"] = np.ascontiguousarray(
        wqkv[:4096].reshape(32, 128, 16, 128).transpose(0, 3, 2, 1))
    d["wv"] = np.ascontiguousarray(
        wqkv[4096:].reshape(16, 128, 16, 128).transpose(0, 3, 2, 1))
    d["wo"] = np.ascontiguousarray(wo.T.reshape(16, 128, 2048))
    d["wffn"] = np.ascontiguousarray(wffn.T.reshape(16, 128, 2048))
    d["wg"] = np.ascontiguousarray(wg.T.reshape(16, 128, 8).transpose(1, 0, 2))
    d["wa"] = np.ascontiguousarray(
        wa.transpose(2, 0, 1).reshape(16, 128, 128).transpose(1, 0, 2))
    d["wb"] = np.ascontiguousarray(wb.transpose(0, 2, 1).reshape(128, 2048))

    r = np.arange(4)[:, None, None]
    i = np.arange(128)[None, :, None]
    j = np.arange(512)[None, None, :]
    d["cmask"] = ((r * 128 + i) <= j).astype(ml_dtypes.bfloat16)
    d["id16"] = np.eye(128, dtype=ml_dtypes.bfloat16)
    d["id32"] = np.eye(128, dtype=np.float32)
    ee = np.zeros((8, 128), dtype=np.float32)
    for e_ in range(8):
        ee[e_, e_ * 16:(e_ + 1) * 16] = 16.0
    d["eexp"] = ee
    d["onescol"] = np.ones((128, 1), dtype=ml_dtypes.bfloat16)
    d["onesrow"] = np.ones((1, 128), dtype=np.float32)
    return d


def kernel(**inputs):
    x = np.ascontiguousarray(inputs["L0_attn_hidden_states"], dtype=np.float32)
    shared = _prep(inputs)
    nc = build_nc()
    in_maps = []
    for c in range(8):
        m = dict(shared)
        m["x"] = np.ascontiguousarray(x[c])
        in_maps.append(m)
    res = run_bass_kernel_spmd(nc, in_maps, core_ids=list(range(8)))
    return np.stack([np.asarray(r["out"]) for r in res.results], axis=0)


# version 04
# speedup vs baseline: 1.1596x; 1.1596x over previous
import contextlib
import math

_LAST_BLD = None

import numpy as np
import ml_dtypes
import concourse.bass as bass
import concourse.mybir as mybir
from concourse.bass_utils import run_bass_kernel_spmd

F32 = mybir.dt.float32
BF16 = mybir.dt.bfloat16
AF = mybir.ActivationFunctionType
ALU = mybir.AluOpType
AX = mybir.AxisListType

S = 2048
H = 2048
NH = 16
KC = 16
NTQ = 16
EPS = 1e-5
S8 = 1.0 / math.sqrt(128.0)
INV_H = 1.0 / float(H)

GLIST = []
GBASE = {}
for _s in range(4):
    for _h in range(NH):
        GBASE[(_s, _h)] = len(GLIST)
        for _g in range(2 * (_s + 1)):
            GLIST.append((_s, _h, _g))


class _MockInstr:
    def then_inc(self, *a, **k):
        return self


class _MockEng:
    def __getattr__(self, name):
        return lambda *a, **k: _MockInstr()


class B:

    def __init__(self, sems):
        self.sems = sems
        self.mode = "plan"
        self.ticks = {}
        self.cnt = {}
        self.events = []
        self.cur_eng = None

    def reset(self, mode):
        self.mode = mode
        self.cnt = {k: 0 for k in self.sems}

    def inc(self, instr, sem, label, n=1):
        self.cnt[sem] += n
        if self.mode == "plan":
            self.ticks[(sem, label)] = self.cnt[sem]
        else:
            assert self.ticks.get((sem, label)) == self.cnt[sem], (sem, label)
            instr.then_inc(self.sems[sem], n)
            self.events.append((self.cur_eng, "inc", sem, n, label))
        return self.cnt[sem]

    def w(self, eng, sem, label):
        if self.mode == "emit":
            v = self.ticks.get((sem, label))
            assert v is not None, ("missing tick", sem, label)
            eng.wait_ge(self.sems[sem], v)
            self.events.append((self.cur_eng, "wait", sem, v, label))


def build_nc(debug=False, phases="a1,a2,a3,b,c"):
    nc = bass.Bass("TRN2", target_bir_lowering=False)

    x_d = nc.dram_tensor("x", [S, H], F32, kind="ExternalInput").ap()
    wqk_d = nc.dram_tensor("wqk", [32, 128, KC, 128], F32, kind="ExternalInput").ap()
    wv_d = nc.dram_tensor("wv", [16, 128, KC, 128], F32, kind="ExternalInput").ap()
    wo_d = nc.dram_tensor("wo", [KC, 128, H], F32, kind="ExternalInput").ap()
    wffn_d = nc.dram_tensor("wffn", [KC, 128, H], F32, kind="ExternalInput").ap()
    wg_d = nc.dram_tensor("wg", [128, KC, 8], F32, kind="ExternalInput").ap()
    wa_d = nc.dram_tensor("wa", [128, KC, 128], F32, kind="ExternalInput").ap()
    wb_d = nc.dram_tensor("wb", [128, H], F32, kind="ExternalInput").ap()
    tmask_d = nc.dram_tensor("tmask", [128, 128], BF16, kind="ExternalInput").ap()
    id16_d = nc.dram_tensor("id16", [128, 128], BF16, kind="ExternalInput").ap()
    id32_d = nc.dram_tensor("id32", [128, 128], F32, kind="ExternalInput").ap()
    eexp_d = nc.dram_tensor("eexp", [8, 128], F32, kind="ExternalInput").ap()
    onescol_d = nc.dram_tensor("onescol", [128, 1], BF16, kind="ExternalInput").ap()
    onesrow_d = nc.dram_tensor("onesrow", [1, 128], F32, kind="ExternalInput").ap()

    out_d = nc.dram_tensor("out", [S, H], F32, kind="ExternalOutput").ap()

    skind = "ExternalOutput" if debug else "Internal"
    import os as _os
    dump_b0 = _os.environ.get("DUMPB0") == "1"
    if dump_b0:
        dbg0_d = nc.dram_tensor("dbg0", [128, 8192], BF16, kind="ExternalOutput").ap()
        dbgl_d = nc.dram_tensor("dbgl", [1, 512], F32, kind="ExternalOutput").ap()
        dbgr_d = nc.dram_tensor("dbgr", [128, 512], BF16, kind="ExternalOutput").ap()
        dbgv_d = nc.dram_tensor("dbgv", [128, 2048], BF16, kind="ExternalOutput").ap()
        dbga_d = nc.dram_tensor("dbga", [128, 512], BF16, kind="ExternalOutput").ap()
    qs_d = nc.dram_tensor("qs", [NH, 128, S], BF16, kind=skind).ap()
    ks_d = nc.dram_tensor("ks", [NH, 128, S], BF16, kind=skind).ap()
    vs_d = nc.dram_tensor("vs", [NH, 128, 2048], BF16, kind=skind).ap()
    x1s_d = nc.dram_tensor("x1s", [NTQ, 128, H], F32, kind=skind).ap()

    with contextlib.ExitStack() as top:
        E = top.enter_context
        arena1 = E(nc.sbuf_tensor("arena1", [128, 32768], BF16))
        arena2 = E(nc.sbuf_tensor("arena2", [128, 32768], BF16))
        af32 = E(nc.sbuf_tensor("af32", [128, 10240], F32))
        a3 = E(nc.sbuf_tensor("a3", [128, 14336], BF16))
        stats = E(nc.sbuf_tensor("stats", [128, 64], F32))
        eps_sb = E(nc.sbuf_tensor("eps_sb", [128, 1], F32))
        id16 = E(nc.sbuf_tensor("id16s", [128, 128], BF16))
        id32 = E(nc.sbuf_tensor("id32s", [128, 128], F32))
        eexp = E(nc.sbuf_tensor("eexps", [8, 128], F32))
        onescol = E(nc.sbuf_tensor("onescols", [128, 1], BF16))
        onesrow = E(nc.sbuf_tensor("onesrows", [1, 128], F32))
        wa_sb = E(nc.sbuf_tensor("wa_sb", [128, 2048], BF16))
        wb_sb = E(nc.sbuf_tensor("wb_sb", [128, 2048], BF16))
        wg_sb = E(nc.sbuf_tensor("wg_sb", [128, 128], BF16))
        tmask_sb = E(nc.sbuf_tensor("tmask_sb", [128, 128], BF16))
        zeros_sb = E(nc.sbuf_tensor("zeros_sb", [128, 128], BF16))

        def xn_fm(kc):
            return arena1[:, kc * 2048:(kc + 1) * 2048]

        def attn_sl(sp, h):
            return arena1[:, sp * 8192 + h * 512: sp * 8192 + (h + 1) * 512]

        def kb(h):
            hp = h % 2
            return arena1[:, 16384 + hp * 2048: 16384 + (hp + 1) * 2048]

        def vb(h):
            hp = h % 2
            return arena1[:, 20480 + hp * 2048: 20480 + (hp + 1) * 2048]

        def qb(h):
            hp = h % 2
            return arena1[:, 24576 + hp * 512: 24576 + (hp + 1) * 512]

        def ptg(slot):
            return arena1[:, 25600 + slot * 1024: 25600 + (slot + 1) * 1024]

        def xn3(sp, kc):
            return arena1[:, sp * 8192 + kc * 512: sp * 8192 + (kc + 1) * 512]

        def w2(kc):
            return arena2[:, kc * 2048:(kc + 1) * 2048]

        def xt3(i):
            i = i % 3
            return af32[:, i * 2048:(i + 1) * 2048]

        def wo_sb(q):
            q = q % 2
            return af32[:, q * 2048:(q + 1) * 2048]

        def xtb(i):
            i = i % 2
            return af32[:, 4096 + i * 2048: 4096 + (i + 1) * 2048]

        logit_sb = af32[0:8, 8192:8704]
        rrow = af32[0:1, 8192:9216]
        lrow = af32[0:1, 9216:10240]
        cwT_sb = af32[0:8, 8704:9216]
        ge_sb = af32[:, 9216:9248]
        gt1_sb = af32[:, 9248:9280]
        gt2_sb = af32[:, 9280:9312]
        cw_sb = af32[:, 9312:9344]
        gm_sb = af32[:, 9344:9360]
        cw_er_sb = af32[:, 9728:10240]

        def slab(i):
            i = i % 2
            return a3[:, i * 2048:(i + 1) * 2048]

        def stage(i):
            i = i % 2
            return a3[:, 4096 + i * 2048: 4096 + (i + 1) * 2048]

        def xn_tm(i):
            i = i % 2
            return a3[:, 8192 + i * 2048: 8192 + (i + 1) * 2048]

        dump = a3[:, 12288:14336]

        def recipB(h):
            hp = h % 2
            return a3[:, 2048 + hp * 512: 2048 + (hp + 1) * 512]

        def t16_sl(sp):
            return a3[:, 4096 + sp * 512: 4096 + (sp + 1) * 512]

        def scol(base, i):
            return stats[:, base + (i % 8): base + (i % 8) + 1]

        CS, CQ, CM, CV, CR, CL = 0, 8, 16, 24, 32, 40

        sems = {}
        for name in ["ld", "w", "st", "pe", "act", "dve"]:
            sems[name] = E(nc.semaphore(name))
        bld = B(sems)
        global _LAST_BLD
        _LAST_BLD = bld
        blk = E(nc.Block())

        def section(eng_name, fn):
            bld.cur_eng = eng_name
            if bld.mode == "plan":
                fn(_MockEng())
            else:
                def wrapped(e, fn=fn, eng_name=eng_name):
                    bld.cur_eng = eng_name
                    fn(e)
                getattr(blk, eng_name)(wrapped)

        def program():
            def boot_sync(e):
                for nm, dst, src in [("id16", id16, id16_d), ("id32", id32, id32_d),
                                     ("eexp", eexp, eexp_d), ("onescol", onescol, onescol_d),
                                     ("onesrow", onesrow, onesrow_d),
                                     ("tmask", tmask_sb, tmask_d)]:
                    bld.inc(e.dma_start(out=dst[:], in_=src), "ld", ("boot", nm), 16)

            def boot_gp(e):
                bld.inc(e.dma_start(out=wa_sb[:], in_=wa_d.rearrange("p k e -> p (k e)")),
                        "w", ("boot", "wa"), 16)
                bld.inc(e.dma_start(out=wb_sb[:], in_=wb_d), "w", ("boot", "wb"), 16)
                bld.inc(e.dma_start(out=wg_sb[:], in_=wg_d.rearrange("p k e -> p (k e)")),
                        "w", ("boot", "wg"), 16)

            def boot_dve(e):
                bld.inc(e.memset(eps_sb[:], EPS), "dve", ("boot", "eps"))
                bld.inc(e.memset(zeros_sb[:], 0.0), "dve", ("boot", "zeros"))

            section("sync", boot_sync)
            section("gpsimd", boot_gp)
            section("vector", boot_dve)

            def fin(e):
                e.wait_ge(sems["st"], bld.cnt["st"])
                if bld.mode == "emit":
                    bld.events.append((bld.cur_eng, "wait", "st", bld.cnt["st"],
                                       ("fin",)))

            def ln_sc_sq(e, tag, i, src):
                ins = e.activation(out=dump[:], in_=src, func=AF.Square,
                                   accum_out=scol(CQ, i))
                bld.inc(ins, "act", (tag, "sq", i))

            def ln_sc_rstd(e, tag, i):
                bld.w(e, "dve", (tag, "var", i))
                e.activation(out=scol(CL, i), in_=scol(CV, i), func=AF.Ln, bias=eps_sb[:])
                ins = e.activation(out=scol(CR, i), in_=scol(CL, i), func=AF.Exp, scale=-0.5)
                bld.inc(ins, "act", (tag, "rstd", i))

            def ln_dve_sum(e, tag, i, src):
                ins = e.reduce_sum(scol(CS, i), src, axis=AX.X)
                bld.inc(ins, "dve", (tag, "sumr", i))
                bld.w(e, "dve", (tag, "sumr", i))
                ins = e.tensor_single_scalar(scol(CM, i), scol(CS, i), -INV_H, ALU.mult)
                bld.inc(ins, "dve", (tag, "m", i))

            def ln_dve_var(e, tag, i):
                bld.w(e, "act", (tag, "sq", i))
                bld.w(e, "dve", (tag, "m", i))
                ins = e.tensor_single_scalar(scol(CV, i), scol(CQ, i), INV_H, ALU.mult)
                bld.inc(ins, "dve", (tag, "v1", i))
                ins = e.tensor_mul(scol(CL, i), scol(CM, i), scol(CM, i))
                bld.inc(ins, "dve", (tag, "v2", i))
                bld.w(e, "dve", (tag, "v1", i))
                bld.w(e, "dve", (tag, "v2", i))
                ins = e.tensor_sub(scol(CV, i), scol(CV, i), scol(CL, i))
                bld.inc(ins, "dve", (tag, "var", i))

            def ln_dve_apply(e, tag, i, src, dst):
                bld.w(e, "act", (tag, "rstd", i))
                bld.w(e, "dve", (tag, "m", i))
                ins = e.tensor_scalar(out=dst, in0=src, scalar1=scol(CM, i),
                                      scalar2=scol(CR, i), op0=ALU.add, op1=ALU.mult)
                bld.inc(ins, "dve", (tag, "apply", i))

            if "a1" not in phases:
                section("sync", fin)
                return
            with contextlib.ExitStack() as ps:
                ptr = [ps.enter_context(
                    nc.psum_tensor(f"ptr{i}_{bld.mode}", [128, 2048], BF16))
                    for i in range(2)]
                pwarm = ps.enter_context(
                    nc.psum_tensor("pwarm_" + bld.mode, [128, 128], F32))

                def a1_sync(e):
                    for t in range(NTQ):
                        if t >= 3:
                            bld.w(e, "dve", ("a1", "apply", t - 3))
                        ins = e.dma_start(out=xt3(t)[:, :], in_=x_d[t * 128:(t + 1) * 128, :])
                        bld.inc(ins, "ld", ("a1x", t), 16)

                def a1_sc(e):
                    bld.w(e, "dve", ("boot", "eps"))
                    for tt in range(NTQ + 1):
                        if tt <= NTQ - 1:
                            bld.w(e, "ld", ("a1x", tt))
                            ln_sc_sq(e, "a1", tt, xt3(tt)[:, :])
                        if 1 <= tt <= NTQ:
                            ln_sc_rstd(e, "a1", tt - 1)

                def a1_dve(e):
                    for tt in range(NTQ + 4):
                        if 4 <= tt:
                            t4 = tt - 4
                            bld.w(e, "pe", ("a1", "tr", t4))
                            srcp = ptr[t4 % 2][:].rearrange("p (k t) -> p k t", k=16)
                            dst = arena1[:, 0:32768].rearrange(
                                "p (k t) -> p k t", k=16)[:, :, t4 * 128:(t4 + 1) * 128]
                            ins = e.tensor_copy(out=dst, in_=srcp)
                            bld.inc(ins, "dve", ("a1", "trevac", t4))
                        if tt <= NTQ - 1:
                            bld.w(e, "ld", ("a1x", tt))
                            ln_dve_sum(e, "a1", tt, xt3(tt)[:, :])
                        if 1 <= tt <= NTQ:
                            ln_dve_var(e, "a1", tt - 1)
                        if 2 <= tt <= NTQ + 1:
                            t2 = tt - 2
                            if t2 >= 2:
                                bld.w(e, "pe", ("a1", "tr", t2 - 2))
                            ln_dve_apply(e, "a1", t2, xt3(t2)[:, :], xn_tm(t2)[:, :])

                def a1_pe(e):
                    bld.w(e, "ld", ("boot", "id16"))
                    for tt in range(2, NTQ + 2):
                        t2 = tt - 2
                        if tt <= NTQ - 1:
                            bld.w(e, "dve", ("a1", "sumr", tt))
                            e.matmul(pwarm[:], id16[:], id16[:],
                                     start=True, stop=True)
                        if tt - 1 <= NTQ - 1:
                            bld.w(e, "dve", ("a1", "v1", tt - 1))
                            e.matmul(pwarm[:], id16[:], id16[:],
                                     start=True, stop=True)
                        bld.w(e, "dve", ("a1", "apply", t2))
                        if t2 >= 2:
                            bld.w(e, "dve", ("a1", "trevac", t2 - 2))
                        for kc in range(KC):
                            ins = e.transpose(ptr[t2 % 2][:, kc * 128:(kc + 1) * 128],
                                              xn_tm(t2)[:, kc * 128:(kc + 1) * 128], id16[:])
                        bld.inc(ins, "pe", ("a1", "tr", t2))

                section("sync", a1_sync)
                section("scalar", a1_sc)
                section("vector", a1_dve)
                section("tensor", a1_pe)

            if "a2" not in phases:
                section("sync", fin)
                return
            with contextlib.ExitStack() as ps:
                pa = [ps.enter_context(
                    nc.psum_tensor(f"pa{i}_{bld.mode}", [128, 2048], F32))
                    for i in range(2)]

                def a2_gp(e):
                    for oc in range(32):
                        if oc >= 2:
                            bld.w(e, "pe", ("a2", "mm", oc - 2))
                        src = wqk_d[oc].rearrange("p k o -> p (k o)")
                        bld.inc(e.dma_start(out=slab(oc)[:, :], in_=src),
                                "w", ("a2slab", oc), 16)

                def a2_pe(e):
                    for oc in range(32):
                        bld.w(e, "w", ("a2slab", oc))
                        if oc >= 2:
                            bld.w(e, "act", ("a2", "evac", oc - 2))
                        for st4 in range(4):
                            if oc == 0:
                                bld.w(e, "dve", ("a1", "trevac", 4 * st4 + 3))
                            for kc in range(KC):
                                ins = e.matmul(pa[oc % 2][:, st4 * 512:(st4 + 1) * 512],
                                               slab(oc)[:, kc * 128:(kc + 1) * 128],
                                               xn_fm(kc)[:, st4 * 512:(st4 + 1) * 512],
                                               start=(kc == 0), stop=(kc == KC - 1))
                        bld.inc(ins, "pe", ("a2", "mm", oc))

                def a2_sc(e):
                    for oc in range(32):
                        bld.w(e, "pe", ("a2", "mm", oc))
                        if oc >= 2:
                            bld.w(e, "st", ("a2st", oc - 2))
                        ins = e.activation(out=stage(oc)[:, :], in_=pa[oc % 2][:],
                                           func=AF.Copy)
                        bld.inc(ins, "act", ("a2", "evac", oc))
                        bld.w(e, "act", ("a2", "evac", oc))
                        dst = qs_d[oc] if oc < 16 else ks_d[oc - 16]
                        bld.inc(e.dma_start(out=dst, in_=stage(oc)[:, :]),
                                "st", ("a2st", oc), 16)

                section("gpsimd", a2_gp)
                section("tensor", a2_pe)
                section("scalar", a2_sc)

            if "a3" not in phases:
                section("sync", fin)
                return
            with contextlib.ExitStack() as ps:
                pa3 = [ps.enter_context(
                    nc.psum_tensor(f"pa3{i}_{bld.mode}", [128, 1024], F32))
                    for i in range(2)]
                vtr = [ps.enter_context(
                    nc.psum_tensor(f"vtr{i}_{bld.mode}", [128, 2048], BF16))
                    for i in range(2)]

                def a3_gp(e):
                    for dc in range(16):
                        if dc < 2:
                            bld.w(e, "pe", ("a2", "mm", 30 + dc))
                        else:
                            bld.w(e, "pe", ("a3", "mm", dc - 2, 1))
                        src = wv_d[dc].rearrange("p k o -> p (k o)")
                        bld.inc(e.dma_start(out=slab(dc)[:, :], in_=src),
                                "w", ("a3slab", dc), 16)
                    for kc in range(KC):
                        bld.inc(e.dma_start(out=w2(kc)[:, :], in_=wo_d[kc]),
                                "w", ("wo", kc), 16)

                def a3_pe(e):
                    bld.w(e, "act", ("a2", "evac", 31))
                    for dcx in range(17):
                        if dcx <= 15:
                            dc = dcx
                            bld.w(e, "w", ("a3slab", dc))
                            for sub in range(2):
                                gsub = dc * 2 + sub
                                gg = gsub % 2
                                if gsub >= 2:
                                    bld.w(e, "act", ("a3", "evac", (gsub - 2) // 2,
                                                     (gsub - 2) % 2))
                                for kc in range(KC):
                                    for st in range(2):
                                        ins = e.matmul(
                                            pa3[gg][:, st * 512:(st + 1) * 512],
                                            slab(dc)[:, kc * 128:(kc + 1) * 128],
                                            xn_fm(kc)[:, (sub * 2 + st) * 512:(sub * 2 + st + 1) * 512],
                                            start=(kc == 0), stop=(kc == KC - 1))
                                bld.inc(ins, "pe", ("a3", "mm", dc, sub))
                        if dcx >= 1:
                            dp = dcx - 1
                            bld.w(e, "act", ("a3", "evac", dp, 1))
                            if dp >= 2:
                                bld.w(e, "act", ("a3", "vtrevac", dp - 2))
                            for tkc in range(16):
                                ins = e.transpose(vtr[dp % 2][:, tkc * 128:(tkc + 1) * 128],
                                                  stage(dp)[:, tkc * 128:(tkc + 1) * 128],
                                                  id16[:])
                            bld.inc(ins, "pe", ("a3", "tr", dp))

                def a3_sc(e):
                    for dcx in range(17):
                        if dcx <= 15:
                            dc = dcx
                            for sub in range(2):
                                bld.w(e, "pe", ("a3", "mm", dc, sub))
                                ins = e.activation(
                                    out=stage(dc)[:, sub * 1024:(sub + 1) * 1024],
                                    in_=pa3[(dc * 2 + sub) % 2][:], func=AF.Copy)
                                bld.inc(ins, "act", ("a3", "evac", dc, sub))
                        if dcx >= 1:
                            dp = dcx - 1
                            bld.w(e, "pe", ("a3", "tr", dp))
                            if dp >= 2:
                                bld.w(e, "st", ("a3vst", dp - 2))
                            ins = e.activation(out=xn_tm(dp)[:, :], in_=vtr[dp % 2][:],
                                               func=AF.Copy)
                            bld.inc(ins, "act", ("a3", "vtrevac", dp))
                            bld.w(e, "act", ("a3", "vtrevac", dp))
                            ins = e.dma_start(out=vs_d[dp], in_=xn_tm(dp)[:, :])
                            bld.inc(ins, "st", ("a3vst", dp), 16)

                section("gpsimd", a3_gp)
                section("tensor", a3_pe)
                section("scalar", a3_sc)

            if "b" not in phases:
                section("sync", fin)
                return
            with contextlib.ExitStack() as ps:
                pscW = ps.enter_context(nc.psum_tensor("pscW_" + bld.mode, [128, 2048], F32))
                pav = [ps.enter_context(
                    nc.psum_tensor(f"pav{i}_{bld.mode}", [128, 512], F32))
                    for i in range(2)]
                psl = ps.enter_context(nc.psum_tensor("psl_" + bld.mode, [1, 512], F32))
                psb = ps.enter_context(nc.psum_tensor("psb_" + bld.mode, [128, 512], F32))

                psc_last = [None, None]

                def b_sync(e, s):
                    nblk = 4 * (s + 1)
                    for h in range(NH):
                        gh = s * NH + h
                        if gh >= 2:
                            ps_, hh = divmod(gh - 2, NH)
                            ng = 2 * (ps_ + 1)
                            bld.w(e, "pe", ("b", "lav", ps_, hh, ng - 1))
                        else:
                            bld.w(e, "pe", ("a3", "mm", 15, 1))
                        if s == 0:
                            bld.w(e, "st", ("a2st", 16 + h))
                            bld.w(e, "st", ("a3vst", min(h, 15)))
                        if dump_b0 and s == 0 and h == 3:
                            bld.inc(e.dma_start(out=dbgv_d, in_=vb(1)[:, :]),
                                    "st", ("dbgv",), 16)
                            bld.w(e, "st", ("dbgv",))
                        ins = e.dma_start(out=kb(h)[:, 0:nblk * 128],
                                          in_=ks_d[h][:, 0:nblk * 128])
                        bld.inc(ins, "ld", ("kb", s, h), 16)
                        ins = e.dma_start(out=qb(h)[:, :],
                                          in_=qs_d[h][:, s * 512:(s + 1) * 512])
                        bld.inc(ins, "ld", ("qb", s, h), 16)
                        ins = e.dma_start(out=vb(h)[:, 0:nblk * 128],
                                          in_=vs_d[h][:, 0:nblk * 128])
                        bld.inc(ins, "ld", ("vb", s, h), 16)

                def b_gp(e, s):
                    for q in range(4):
                        gq = s * 4 + q
                        if gq >= 2:
                            bld.w(e, "dve", ("b2", "x1add", (gq - 2) // 4, (gq - 2) % 4))
                            bld.w(e, "st", ("b2", "x1st", (gq - 2) // 4, (gq - 2) % 4))
                        else:
                            bld.w(e, "dve", ("a1", "apply", NTQ - 1))
                        row = s * 512 + q * 128
                        ins = e.dma_start(out=xtb(gq)[:, :], in_=x_d[row:row + 128, :])
                        bld.inc(ins, "w", ("xb2", s, q), 16)

                def b_pe(e, s):
                    nblk = 4 * (s + 1)
                    NG = nblk // 2
                    if s == 0:
                        bld.w(e, "act", ("a3", "evac", 15, 1))
                        bld.w(e, "act", ("a3", "vtrevac", 15))
                        bld.w(e, "ld", ("boot", "tmask"))

                    def emit_sc(h, g):
                        gg = g % 2
                        diag = (g >= NG - 2)
                        if psc_last[gg] is not None:
                            bld.w(e, "act", psc_last[gg])
                        for c2 in range(2):
                            c = 2 * g + c2
                            base = gg * 1024 + c2 * 512
                            ins = e.matmul(
                                pscW[:, base: base + 512],
                                kb(h)[:, c * 128:(c + 1) * 128],
                                qb(h)[:, :], start=True, stop=not diag)
                            if diag:
                                r = c - 4 * s
                                ins = e.matmul(
                                    pscW[:, base + r * 128: base + (r + 1) * 128],
                                    id16[:], tmask_sb[:], start=False, stop=True)
                        bld.inc(ins, "pe", ("b", "sc", s, h, g))
                        psc_last[gg] = ("b", "exp", s, h, g)

                    def emit_lav(h, g):
                        gh = s * NH + h
                        base = GBASE[(s, h)]
                        if g == 0:
                            bld.w(e, "ld", ("vb", s, h))
                            if gh >= 1:
                                ph = gh - 1
                                bld.w(e, "act", ("b", "lev", ph // NH, ph % NH))
                            if gh >= 2:
                                ph = gh - 2
                                bld.w(e, "act", ("b", "attnev", ph // NH, ph % NH))
                        diag = (g >= NG - 2)
                        bld.w(e, "act", ("b", "exp", s, h, g))
                        slot = (base + g) % 4
                        for c2 in range(2):
                            c = 2 * g + c2
                            lo = (c - 4 * s) * 128 if diag else 0
                            e.matmul(psl[:, lo:512], onescol[:],
                                     ptg(slot)[:, c2 * 512 + lo:(c2 + 1) * 512],
                                     start=(c == 0), stop=(c == nblk - 1))
                            ins = e.matmul(pav[h % 2][:, lo:512],
                                           vb(h)[:, c * 128:(c + 1) * 128],
                                           ptg(slot)[:, c2 * 512 + lo:(c2 + 1) * 512],
                                           start=(c == 0), stop=(c == nblk - 1))
                        bld.inc(ins, "pe", ("b", "lav", s, h, g))

                    def emit_bc(h):
                        gh = s * NH + h
                        bld.w(e, "dve", ("b", "recip", s, h))
                        if h >= 2:
                            bld.w(e, "dve", ("b", "amul", s, h - 2))
                        else:
                            bld.w(e, "dve", ("b", "recip", s, h + 1))
                        if gh >= 1:
                            ph = gh - 1
                            bld.w(e, "act", ("b", "rbevac", ph // NH, ph % NH))
                        ins = e.matmul(psb[:, :], onesrow[:],
                                       rrow[:, (h % 2) * 512:((h % 2) + 1) * 512],
                                       start=True, stop=True)
                        bld.inc(ins, "pe", ("b", "bc", s, h))

                    for h in range(NH):
                        bld.w(e, "ld", ("kb", s, h))
                        bld.w(e, "ld", ("qb", s, h))
                        emit_sc(h, 0)
                        if NG > 1:
                            emit_sc(h, 1)
                        if h > 0:
                            emit_lav(h - 1, NG - 2)
                            emit_lav(h - 1, NG - 1)
                        if h > 1:
                            emit_bc(h - 2)
                        for gx in range(2, NG):
                            emit_sc(h, gx)
                            emit_lav(h, gx - 2)
                    emit_lav(NH - 1, NG - 2)
                    emit_lav(NH - 1, NG - 1)
                    emit_bc(NH - 2)
                    emit_bc(NH - 1)
                    if s == 3:
                        bld.w(e, "dve", ("boot", "zeros"))
                        e.matmul(pav[0][:, 0:384], zeros_sb[:],
                                 kb(0)[:, 0:384], start=False, stop=True)
                        e.matmul(pav[1][:, 0:384], zeros_sb[:],
                                 kb(0)[:, 0:384], start=False, stop=True)
                        e.matmul(psl[:, 0:384], zeros_sb[:, 0:1],
                                 kb(0)[:, 0:384], start=False, stop=True)
                    bld.w(e, "dve", ("b", "amul", s, NH - 1))
                    if s == 0:
                        bld.w(e, "w", ("wo", KC - 1))
                    for q in range(4):
                        for os_ in range(4):
                            if q == 0:
                                if os_ % 2 == 0 and psc_last[os_ // 2] is not None:
                                    bld.w(e, "act", psc_last[os_ // 2])
                            else:
                                bld.w(e, "act", ("b2", "evac", s, q - 1,
                                                 2 * (os_ // 2) + 1))
                            for kc in range(KC):
                                ins = e.matmul(
                                    pscW[:, os_ * 512:(os_ + 1) * 512],
                                    attn_sl(s % 2, kc)[:, q * 128:(q + 1) * 128],
                                    w2(kc)[:, os_ * 512:(os_ + 1) * 512],
                                    start=(kc == 0), stop=(kc == KC - 1))
                            bld.inc(ins, "pe", ("b", "womm", s, q, os_))
                    psc_last[0] = ("b2", "evac", s, 3, 1)
                    psc_last[1] = ("b2", "evac", s, 3, 3)

                def b_sc(e, s):
                    nblk = 4 * (s + 1)
                    NG = nblk // 2

                    def emit_rbevac(hh):
                        gh_ = s * NH + hh
                        bld.w(e, "pe", ("b", "bc", s, hh))
                        if gh_ >= 2:
                            bld.w(e, "dve", ("b", "amul", (gh_ - 2) // NH, (gh_ - 2) % NH))
                        ins_ = e.activation(out=recipB(hh)[:, :], in_=psb[:], func=AF.Copy)
                        bld.inc(ins_, "act", ("b", "rbevac", s, hh))
                        if dump_b0 and s == 0 and hh == 1:
                            bld.w(e, "act", ("b", "rbevac", 0, 1))
                            bld.inc(e.dma_start(out=dbgr_d, in_=recipB(1)[:, :]),
                                    "st", ("dbgr",), 16)

                    for h in range(NH):
                        gh = s * NH + h
                        base = GBASE[(s, h)]
                        if h == 0 and s >= 2:
                            bld.w(e, "pe", ("b", "womm", s - 2, 3, 3))
                        for g in range(NG):
                            bld.w(e, "pe", ("b", "sc", s, h, g))
                            gi = base + g
                            if gi >= 4:
                                bld.w(e, "pe", ("b", "lav") + GLIST[gi - 4])
                            slot = gi % 4
                            gg = g % 2
                            ins = e.activation(out=ptg(slot)[:, :],
                                               in_=pscW[:, gg * 1024:(gg + 1) * 1024],
                                               func=AF.Exp, scale=S8)
                            bld.inc(ins, "act", ("b", "exp", s, h, g))
                        bld.w(e, "pe", ("b", "lav", s, h, NG - 1))
                        gh0 = s * NH + h
                        if gh0 >= 2:
                            bld.w(e, "dve", ("b", "recip", (gh0 - 2) // NH, (gh0 - 2) % NH))
                        ins = e.activation(out=lrow[:, (h % 2) * 512:((h % 2) + 1) * 512],
                                           in_=psl[:], func=AF.Copy)
                        bld.inc(ins, "act", ("b", "lev", s, h))
                        ins = e.activation(out=attn_sl(s % 2, h)[:, :], in_=pav[h % 2][:],
                                           func=AF.Copy)
                        bld.inc(ins, "act", ("b", "attnev", s, h))
                        if dump_b0 and s == 0 and h == 1:
                            bld.w(e, "act", ("b", "lev", 0, 1))
                            bld.inc(e.dma_start(out=dbgl_d,
                                                in_=lrow[:, 512:1024]),
                                    "st", ("dbgl",), 16)
                            bld.w(e, "act", ("b", "attnev", 0, 1))
                            bld.inc(e.dma_start(out=dbga_d,
                                                in_=attn_sl(0, 1)[:, :]),
                                    "st", ("dbga",), 16)
                        if h >= 2:
                            emit_rbevac(h - 2)
                    for hx in (NH - 2, NH - 1):
                        emit_rbevac(hx)
                    for qq in range(5):
                        if qq <= 3:
                            q = qq
                            gq = s * 4 + q
                            for os_ in range(4):
                                bld.w(e, "pe", ("b", "womm", s, q, os_))
                                if gq >= 2 and os_ == 0:
                                    bld.w(e, "dve", ("b2", "x1add", (gq - 2) // 4,
                                                     (gq - 2) % 4))
                                ins = e.activation(
                                    out=wo_sb(q)[:, os_ * 512:(os_ + 1) * 512],
                                    in_=pscW[:, os_ * 512:(os_ + 1) * 512], func=AF.Copy)
                                bld.inc(ins, "act", ("b2", "evac", s, q, os_))
                            ln_sc_sq(e, ("b2", s), q, wo_sb(q)[:, :])
                        if qq >= 1:
                            q = qq - 1
                            ln_sc_rstd(e, ("b2", s), q)
                            bld.w(e, "dve", ("b2", "x1add", s, q))
                            gq = s * 4 + q
                            ins = e.dma_start(out=x1s_d[gq], in_=xtb(gq)[:, :])
                            bld.inc(ins, "st", ("b2", "x1st", s, q), 16)

                def b_dve(e, s):
                    nblk = 4 * (s + 1)
                    NG = nblk // 2

                    def emit_amul(hh):
                        bld.w(e, "act", ("b", "attnev", s, hh))
                        bld.w(e, "act", ("b", "rbevac", s, hh))
                        if dump_b0 and s == 0 and hh == 1:
                            bld.w(e, "st", ("dbga",))
                        at_ = attn_sl(s % 2, hh)[:, :]
                        ins_ = e.tensor_mul(at_, at_, recipB(hh)[:, :])
                        bld.inc(ins_, "dve", ("b", "amul", s, hh))

                    for h in range(NH):
                        gh = s * NH + h
                        bld.w(e, "act", ("b", "attnev", s, h))
                        if gh >= 2:
                            ph = gh - 2
                            bld.w(e, "pe", ("b", "bc", ph // NH, ph % NH))
                        ins = e.reciprocal(rrow[:, (h % 2) * 512:((h % 2) + 1) * 512],
                                           lrow[:, (h % 2) * 512:((h % 2) + 1) * 512])
                        bld.inc(ins, "dve", ("b", "recip", s, h))
                        if h >= 2:
                            emit_amul(h - 2)
                    for hx in (NH - 2, NH - 1):
                        emit_amul(hx)
                    for qq in range(5):
                        if qq >= 1:
                            q = qq - 1
                            ln_dve_apply(e, ("b2", s), q, wo_sb(q)[:, :], wo_sb(q)[:, :])
                            bld.w(e, "w", ("xb2", s, q))
                            gq = s * 4 + q
                            ins = e.tensor_add(xtb(gq)[:, :], xtb(gq)[:, :], wo_sb(q)[:, :])
                            bld.inc(ins, "dve", ("b2", "x1add", s, q))
                        if qq <= 3:
                            q = qq
                            bld.w(e, "act", ("b2", "evac", s, q, 3))
                            ln_dve_sum(e, ("b2", s), q, wo_sb(q)[:, :])
                            ln_dve_var(e, ("b2", s), q)

                for s in range(4):
                    section("sync", lambda e, s=s: b_sync(e, s))
                    section("gpsimd", lambda e, s=s: b_gp(e, s))
                    section("tensor", lambda e, s=s: b_pe(e, s))
                    section("scalar", lambda e, s=s: b_sc(e, s))
                    section("vector", lambda e, s=s: b_dve(e, s))
                    if s == 0 and dump_b0:
                        def b0_dump(e):
                            bld.w(e, "dve", ("b", "amul", 0, NH - 1))
                            bld.inc(e.dma_start(out=dbg0_d, in_=arena1[:, 0:8192]),
                                    "st", ("dbg0",), 16)
                        section("sync", b0_dump)

                def b_wffn(e):
                    bld.w(e, "pe", ("b", "womm", 3, 3, 3))
                    for kc in range(KC):
                        bld.inc(e.dma_start(out=w2(kc)[:, :], in_=wffn_d[kc]),
                                "w", ("wffn", kc), 16)

                section("gpsimd", b_wffn)

            if "c" not in phases:
                section("sync", fin)
                return
            with contextlib.ExitStack() as ps:
                pW = ps.enter_context(nc.psum_tensor("pW_" + bld.mode, [128, 2048], F32))
                psA = ps.enter_context(nc.psum_tensor("psA_" + bld.mode, [128, 512], F32))
                pmisc = ps.enter_context(nc.psum_tensor("pmisc_" + bld.mode, [128, 512], F32))
                trC = ps.enter_context(nc.psum_tensor("trC_" + bld.mode, [128, 2048], BF16))

                def c_sync(e, it):
                    sw = it - 1
                    sp4 = it
                    if 0 <= sw:
                        for q in range(4):
                            gq = sw * 4 + q
                            bld.w(e, "dve", (("c3", sw), "apply", 3))
                            if q >= 2:
                                bld.w(e, "dve", ("c4", "outadd", sw, q - 2))
                                bld.w(e, "st", ("out", sw, q - 2))
                            ins = e.dma_start(out=xtb(gq)[:, :], in_=x1s_d[gq])
                            bld.inc(ins, "ld", ("x1b", sw, q), 16)
                    if sp4 <= 3:
                        for q in range(4):
                            gq = sp4 * 4 + q
                            if q >= 2:
                                bld.w(e, "dve", (("c3", sp4), "apply", q - 2))
                            elif sp4 >= 1:
                                bld.w(e, "dve", ("c4", "outadd", sp4 - 1, q + 2))
                                bld.w(e, "st", ("out", sp4 - 1, q + 2))
                            else:
                                bld.w(e, "st", ("b2", "x1st", 3, 3))
                            ins = e.dma_start(out=xtb(gq)[:, :], in_=x1s_d[gq])
                            bld.inc(ins, "ld", ("x1a", sp4, q), 16)

                def c_pe(e, it):
                    sw = it - 1
                    sp4 = it
                    spp = sp4 % 2
                    swp = sw % 2

                    def sp4_block():
                        if sp4 > 3:
                            return
                        if sp4 == 0:
                            bld.w(e, "w", ("boot", "wa"))
                            bld.w(e, "w", ("boot", "wb"))
                            bld.w(e, "w", ("boot", "wg"))
                            bld.w(e, "ld", ("boot", "id32"))
                            bld.w(e, "ld", ("boot", "eexp"))
                            bld.w(e, "act", ("b", "attnev", 3, 15))
                            bld.w(e, "act", ("b", "rbevac", 3, 15))
                            bld.w(e, "dve", ("b", "recip", 3, 15))
                        for q in range(4):
                            bld.w(e, "dve", (("c3", sp4), "apply", q))
                            gq = sp4 * 4 + q
                            if gq >= 1:
                                pq = gq - 1
                                bld.w(e, "dve", ("c3", "trevac", pq // 4, pq % 4))
                            for kc in range(KC):
                                ins = e.transpose(trC[:, kc * 128:(kc + 1) * 128],
                                                  xn_tm(q)[:, kc * 128:(kc + 1) * 128],
                                                  id16[:])
                            bld.inc(ins, "pe", ("c3", "tr", sp4, q))
                        bld.w(e, "dve", ("c3", "trevac", sp4, 3))
                        if sp4 >= 1:
                            bld.w(e, "dve", ("c", "t16", sp4 - 1))
                            bld.w(e, "act", ("c", "cwerevac", sp4 - 1))
                        for kc in range(KC):
                            e.matmul(psA[:, :], wa_sb[:, kc * 128:(kc + 1) * 128],
                                     xn3(spp, kc)[:, :],
                                     start=(kc == 0), stop=(kc == KC - 1))
                            ins = e.matmul(pmisc[0:8, :], wg_sb[:, kc * 8:(kc + 1) * 8],
                                           xn3(spp, kc)[:, :],
                                           start=(kc == 0), stop=(kc == KC - 1))
                        bld.inc(ins, "pe", ("c", "ag", sp4))

                    def gating_piece(which):
                        if sp4 > 3:
                            return
                        if which == 0:
                            bld.w(e, "act", ("c", "logevac", sp4))
                            if sp4 >= 1:
                                bld.w(e, "act", ("c", "geexp", sp4 - 1))
                            for q in range(4):
                                ins = e.transpose(pmisc[:, q * 8:(q + 1) * 8],
                                                  logit_sb[:, q * 128:(q + 1) * 128],
                                                  id32[0:8, 0:8])
                            bld.inc(ins, "pe", ("c", "tr1", sp4))
                        elif which == 1:
                            bld.w(e, "dve", ("c", "gat", sp4))
                            for q in range(4):
                                ins = e.transpose(pmisc[0:8, q * 128:(q + 1) * 128],
                                                  cw_sb[:, q * 8:(q + 1) * 8], id32[:])
                            bld.inc(ins, "pe", ("c", "tr2", sp4))
                        elif which == 2:
                            bld.w(e, "act", ("c", "cwTevac", sp4))
                            for q in range(4):
                                ins = e.matmul(pmisc[:, q * 128:(q + 1) * 128], eexp[:],
                                               cwT_sb[:, q * 128:(q + 1) * 128],
                                               start=True, stop=True)
                            bld.inc(ins, "pe", ("c", "exp", sp4))

                    if 0 <= sw:
                        if sw == 0:
                            bld.w(e, "w", ("wffn", KC - 1))
                        for q in range(4):
                            for os_ in range(4):
                                if q == 0 and sw == 0:
                                    if os_ == 0:
                                        bld.w(e, "act", ("b2", "evac", 3, 3, 3))
                                elif q == 0:
                                    bld.w(e, "act", ("c4", "evac", sw - 1, 3,
                                                     2 * (os_ // 2) + 1))
                                else:
                                    bld.w(e, "act", ("c4", "evac", sw, q - 1,
                                                     2 * (os_ // 2) + 1))
                                for kc in range(KC):
                                    e.matmul(pW[:, os_ * 512:(os_ + 1) * 512],
                                             xn3(swp, kc)[:, q * 128:(q + 1) * 128],
                                             w2(kc)[:, os_ * 512:(os_ + 1) * 512],
                                             start=(kc == 0), stop=False)
                                if q == 0 and os_ == 0:
                                    bld.w(e, "dve", ("c", "t16", sw))
                                ins = e.matmul(pW[:, os_ * 512:(os_ + 1) * 512],
                                               t16_sl(swp)[:, q * 128:(q + 1) * 128],
                                               wb_sb[:, os_ * 512:(os_ + 1) * 512],
                                               start=False, stop=True)
                                bld.inc(ins, "pe", ("c4", "mm", sw, q, os_))
                    sp4_block()
                    gating_piece(0)
                    gating_piece(1)
                    gating_piece(2)

                def c_sc(e, it):
                    sw = it - 1
                    sp4 = it

                    def sp4_block():
                        if sp4 > 3:
                            return
                        for qq in range(5):
                            if qq <= 3:
                                q = qq
                                bld.w(e, "ld", ("x1a", sp4, q))
                                if sp4 == 0 and q == 0:
                                    bld.w(e, "st", ("a3vst", 15))
                                ln_sc_sq(e, ("c3", sp4), q, xtb(sp4 * 4 + q)[:, :])
                            if 1 <= qq <= 4:
                                ln_sc_rstd(e, ("c3", sp4), qq - 1)
                        bld.w(e, "pe", ("c", "ag", sp4))
                        ins = e.activation(out=logit_sb[:, :], in_=pmisc[0:8, :],
                                           func=AF.Copy)
                        bld.inc(ins, "act", ("c", "logevac", sp4))

                    def gate_evac(which):
                        if sp4 > 3:
                            return
                        if which == 0:
                            bld.w(e, "pe", ("c", "tr1", sp4))
                            ins_ = e.activation(out=ge_sb[:, :], in_=pmisc[:, 0:32],
                                                func=AF.Exp)
                            bld.inc(ins_, "act", ("c", "geexp", sp4))
                        elif which == 1:
                            bld.w(e, "pe", ("c", "tr2", sp4))
                            ins_ = e.activation(out=cwT_sb[:, :], in_=pmisc[0:8, :],
                                                func=AF.Copy)
                            bld.inc(ins_, "act", ("c", "cwTevac", sp4))
                        elif which == 2:
                            bld.w(e, "pe", ("c", "exp", sp4))
                            if sp4 >= 1:
                                bld.w(e, "dve", ("c", "t16", sp4 - 1))
                            ins_ = e.activation(out=cw_er_sb[:, :], in_=pmisc[:, :],
                                                func=AF.Copy)
                            bld.inc(ins_, "act", ("c", "cwerevac", sp4))

                    if 0 <= sw:
                        for qq in range(5):
                            if qq <= 3:
                                q = qq
                                for os_ in range(4):
                                    bld.w(e, "pe", ("c4", "mm", sw, q, os_))
                                    if os_ == 0 and q >= 2:
                                        bld.w(e, "dve", ("c4", "outadd", sw, q - 2))
                                    if os_ == 0 and q < 2 and sw >= 1:
                                        bld.w(e, "dve", ("c4", "outadd", sw - 1, q + 2))
                                    ins = e.activation(
                                        out=wo_sb(q)[:, os_ * 512:(os_ + 1) * 512],
                                        in_=pW[:, os_ * 512:(os_ + 1) * 512], func=AF.Copy)
                                    bld.inc(ins, "act", ("c4", "evac", sw, q, os_))
                                ln_sc_sq(e, ("c4", sw), q + 4, wo_sb(q)[:, :])
                            if qq >= 1:
                                q = qq - 1
                                ln_sc_rstd(e, ("c4", sw), q + 4)
                                bld.w(e, "dve", ("c4", "outadd", sw, q))
                                gq = sw * 4 + q
                                row = gq * 128
                                ins = e.dma_start(out=out_d[row:row + 128, :],
                                                  in_=xtb(gq)[:, :])
                                bld.inc(ins, "st", ("out", sw, q), 16)
                    sp4_block()
                    gate_evac(0)
                    gate_evac(1)
                    gate_evac(2)

                def c_dve(e, it):
                    sw = it - 1
                    sp4 = it
                    spp = sp4 % 2
                    if sp4 == 0:
                        bld.w(e, "pe", ("b", "womm", 3, 3, 3))

                    def sp4_block():
                        if sp4 > 3:
                            return
                        for qq in range(6):
                            if 2 <= qq <= 5:
                                q = qq - 2
                                gq = sp4 * 4 + q
                                if gq >= 2:
                                    pq = gq - 2
                                    bld.w(e, "pe", ("c3", "tr", pq // 4, pq % 4))
                                ln_dve_apply(e, ("c3", sp4), q, xtb(sp4 * 4 + q)[:, :],
                                             xn_tm(q)[:, :])
                            if qq <= 3:
                                q = qq
                                bld.w(e, "ld", ("x1a", sp4, q))
                                ln_dve_sum(e, ("c3", sp4), q, xtb(sp4 * 4 + q)[:, :])
                            if 1 <= qq <= 4:
                                ln_dve_var(e, ("c3", sp4), qq - 1)
                            if 2 <= qq:
                                q = qq - 2
                                bld.w(e, "pe", ("c3", "tr", sp4, q))
                                if sp4 >= 2 and q == 0:
                                    bld.w(e, "pe", ("c4", "mm", sp4 - 2, 3, 3))
                                srcp = trC[:].rearrange("p (k t) -> p k t", k=16)
                                dst = arena1[:, spp * 8192:(spp + 1) * 8192].rearrange(
                                    "p (k t) -> p k t", k=16)[:, :, q * 128:(q + 1) * 128]
                                ins = e.tensor_copy(out=dst, in_=srcp)
                                bld.inc(ins, "dve", ("c3", "trevac", sp4, q))
                        bld.w(e, "act", ("c", "geexp", sp4))
                        for q in range(4):
                            ge = ge_sb[:, q * 8:(q + 1) * 8]
                            gm1 = gm_sb[:, q:q + 1]
                            gm2 = gm_sb[:, 4 + q:5 + q]
                            gden = gm_sb[:, 8 + q:9 + q]
                            t1 = gt1_sb[:, q * 8:(q + 1) * 8]
                            t2 = gt2_sb[:, q * 8:(q + 1) * 8]
                            cw = cw_sb[:, q * 8:(q + 1) * 8]

                            def gstep(ins_, lbl):
                                bld.inc(ins_, "dve", ("c", lbl, sp4, q))
                                bld.w(e, "dve", ("c", lbl, sp4, q))
                            gstep(e.reduce_max(gm1, ge, axis=AX.X), "g1")
                            gstep(e.tensor_scalar(out=t1, in0=ge, scalar1=gm1,
                                                  scalar2=None, op0=ALU.is_equal), "g1b")
                            gstep(e.tensor_single_scalar(t1, t1, 1e30, ALU.mult), "g1c")
                            gstep(e.tensor_sub(t1, ge, t1), "g1d")
                            gstep(e.reduce_max(gm2, t1, axis=AX.X), "g2")
                            gstep(e.tensor_scalar(out=t2, in0=ge, scalar1=gm2,
                                                  scalar2=None, op0=ALU.is_ge), "g2b")
                            gstep(e.tensor_mul(cw, ge, t2), "g2c")
                            gstep(e.reduce_sum(gden, cw, axis=AX.X), "g3")
                            gstep(e.reciprocal(gden, gden), "g4")
                            ins = e.tensor_scalar(out=cw, in0=cw, scalar1=gden,
                                                  scalar2=None, op0=ALU.mult)
                        bld.inc(ins, "dve", ("c", "gat", sp4))
                        bld.w(e, "act", ("c", "cwerevac", sp4))
                        if sp4 >= 2:
                            bld.w(e, "pe", ("c4", "mm", sp4 - 2, 3, 3))
                        ins = e.tensor_tensor(out=t16_sl(sp4 % 2)[:, :], in0=psA[:, :],
                                              in1=cw_er_sb[:, :], op=ALU.mult)
                        bld.inc(ins, "dve", ("c", "t16", sp4))

                    if 0 <= sw:
                        for qq in range(5):
                            if qq >= 1:
                                q = qq - 1
                                ln_dve_apply(e, ("c4", sw), q + 4, wo_sb(q)[:, :],
                                             wo_sb(q)[:, :])
                                bld.w(e, "ld", ("x1b", sw, q))
                                gq = sw * 4 + q
                                ins = e.tensor_add(xtb(gq)[:, :], xtb(gq)[:, :],
                                                   wo_sb(q)[:, :])
                                bld.inc(ins, "dve", ("c4", "outadd", sw, q))
                            if qq <= 3:
                                q = qq
                                bld.w(e, "act", ("c4", "evac", sw, q, 3))
                                ln_dve_sum(e, ("c4", sw), q + 4, wo_sb(q)[:, :])
                                ln_dve_var(e, ("c4", sw), q + 4)
                    sp4_block()

                n_it = int(_os.environ.get("CITERS", "5"))
                for it in range(n_it):
                    section("sync", lambda e, it=it: c_sync(e, it))
                    section("tensor", lambda e, it=it: c_pe(e, it))
                    section("scalar", lambda e, it=it: c_sc(e, it))
                    section("vector", lambda e, it=it: c_dve(e, it))

            section("sync", fin)

        bld.reset("plan")
        program()
        bld.reset("emit")
        program()

    return nc


def _prep(inputs):
    wqkv = np.ascontiguousarray(inputs["L0_attn_w_qkv"], dtype=np.float32)
    wo = np.ascontiguousarray(inputs["L0_attn_w_o"], dtype=np.float32)
    wffn = np.ascontiguousarray(inputs["L0_ffn_weight"], dtype=np.float32)
    wg = np.ascontiguousarray(inputs["L0_ffn_gate_w"], dtype=np.float32)
    wa = np.ascontiguousarray(inputs["L0_ffn_A_w"], dtype=np.float32)
    wb = np.ascontiguousarray(inputs["L0_ffn_B_w"], dtype=np.float32)

    d = {}
    d["wqk"] = np.ascontiguousarray(
        wqkv[:4096].reshape(32, 128, 16, 128).transpose(0, 3, 2, 1))
    d["wv"] = np.ascontiguousarray(
        wqkv[4096:].reshape(16, 128, 16, 128).transpose(0, 3, 2, 1))
    d["wo"] = np.ascontiguousarray(wo.T.reshape(16, 128, 2048))
    d["wffn"] = np.ascontiguousarray(wffn.T.reshape(16, 128, 2048))
    d["wg"] = np.ascontiguousarray(wg.T.reshape(16, 128, 8).transpose(1, 0, 2))
    d["wa"] = np.ascontiguousarray(
        wa.transpose(2, 0, 1).reshape(16, 128, 128).transpose(1, 0, 2))
    d["wb"] = np.ascontiguousarray(wb.transpose(0, 2, 1).reshape(128, 2048))

    i = np.arange(128)[:, None]
    j = np.arange(128)[None, :]
    d["tmask"] = np.where(i <= j, 0.0, -30000.0).astype(ml_dtypes.bfloat16)
    d["id16"] = np.eye(128, dtype=ml_dtypes.bfloat16)
    d["id32"] = np.eye(128, dtype=np.float32)
    ee = np.zeros((8, 128), dtype=np.float32)
    for e_ in range(8):
        ee[e_, e_ * 16:(e_ + 1) * 16] = 16.0
    d["eexp"] = ee
    d["onescol"] = np.ones((128, 1), dtype=ml_dtypes.bfloat16)
    d["onesrow"] = np.ones((1, 128), dtype=np.float32)
    return d


def kernel(**inputs):
    x = np.ascontiguousarray(inputs["L0_attn_hidden_states"], dtype=np.float32)
    shared = _prep(inputs)
    nc = build_nc()
    in_maps = []
    for c in range(8):
        m = dict(shared)
        m["x"] = np.ascontiguousarray(x[c])
        in_maps.append(m)
    res = run_bass_kernel_spmd(nc, in_maps, core_ids=list(range(8)))
    return np.stack([np.asarray(r["out"]) for r in res.results], axis=0)
